# version 06
# baseline (speedup 1.0000x reference)
import math
import contextlib
import numpy as np

import concourse.bass as bass
import concourse.mybir as mybir
from concourse.bass_utils import run_bass_kernel_spmd

F32 = mybir.dt.float32
AF = mybir.ActivationFunctionType
ALU = mybir.AluOpType
AX = mybir.AxisListType

P = 128
B, T, C = 4, 2048, 4096
H, HD, HPC = 32, 128, 16
AT = 10
TA = T + AT
NCC = C // P
NQB = T // P
NOC = 3 * HPC
SC_QK = 1.0 / math.sqrt(HD)
SC_AD = 1.0 / math.sqrt(C)
EPS = 1e-5
NEG = -1.0e30
TILE_W = (512, 512, 512, 522)
RG = [[0, 1], [2, 3], [4, 5], [6, 7]]

ENGS = ("sync", "gpsimd", "scalar", "vector", "tensor")


class _Sem:
    __slots__ = ("h", "v")

    def __init__(self, nc, name):
        self.h = nc.alloc_semaphore(name=name)
        self.v = 0


class _IR:

    def __init__(self):
        self.ops = {e: [] for e in ENGS}
        self.auto = {}
        self.stopped = False

    def add(self, eng, fn, waits=()):
        if self.stopped:
            return [None, [], []]
        rec = [fn, [w for w in waits if w is not None and w[1] is not None], []]
        sem = self.auto.get(eng)
        if sem is not None:
            rec[2].append((sem.h, 1))
            sem.v += 1
        self.ops[eng].append(rec)
        return rec

    @staticmethod
    def inc(rec, sem, n=1):
        if rec[0] is None:
            return sem.v
        for (hh, _n) in rec[2]:
            if hh is sem.h:
                return sem.v
        rec[2].append((sem.h, n))
        sem.v += n
        return sem.v


class _Stop(Exception):
    pass


def _build(trunc=9):
    nc = bass.Bass("TRN2", target_bir_lowering=False)
    ir = _IR()
    MM = _Sem(nc, "MM")
    SC = _Sem(nc, "SC")
    VC = _Sem(nc, "VC")
    ir.auto = {"vector": VC, "scalar": SC}
    CCS = _Sem(nc, "CCS")
    LDC = _Sem(nc, "LDC")
    LDX = _Sem(nc, "LDX")
    LDW = [_Sem(nc, f"LDW{i}") for i in range(2)]
    LQ = [_Sem(nc, f"LQ{i}") for i in range(2)]
    LK = [_Sem(nc, f"LK{i}") for i in range(2)]
    LV = [_Sem(nc, f"LV{i}") for i in range(2)]
    LY = [_Sem(nc, f"LY{i}") for i in range(2)]
    LW3 = [_Sem(nc, f"LW3{i}") for i in range(2)]
    LDP = _Sem(nc, "LDP")
    LDX2 = _Sem(nc, "LDX2")
    SW = [_Sem(nc, f"SW{i}") for i in range(2)]
    SP = [_Sem(nc, f"SP{i}") for i in range(2)]
    YS = [_Sem(nc, f"YS{i}") for i in range(2)]
    SG = [_Sem(nc, f"SG{i}") for i in range(4)]
    LXT = [_Sem(nc, f"LXT{i}") for i in range(2)]
    SX = [_Sem(nc, f"SX{i}") for i in range(2)]
    OUT = _Sem(nc, "OUT")

    d_xfm = nc.dram_tensor("xfm3", [P, NCC, TA], F32, kind="ExternalInput")
    d_xtm = nc.dram_tensor("xtm", [1024, C], F32, kind="ExternalInput")
    d_wqkv = nc.dram_tensor("wqkv4", [NOC, P, NCC, P], F32, kind="ExternalInput")
    d_wp = nc.dram_tensor("wp4", [16, 8, P, 512], F32, kind="ExternalInput")
    d_wb = nc.dram_tensor("wb4", [NCC, 8, P, 512], F32, kind="ExternalInput")
    d_war = nc.dram_tensor("war3", [P, NCC, 24], F32, kind="ExternalInput")
    d_wBT = nc.dram_tensor("wBT", [16, C], F32, kind="ExternalInput")
    d_bb = nc.dram_tensor("bb", [1, C], F32, kind="ExternalInput")
    d_cor = nc.dram_tensor("coresr", [16, P], F32, kind="ExternalInput")
    d_gat = nc.dram_tensor("gat", [1, 1], F32, kind="ExternalInput")
    d_ch = nc.dram_tensor("cos_hi", [P, TA], F32, kind="ExternalInput")
    d_cl = nc.dram_tensor("cos_lo", [P, TA], F32, kind="ExternalInput")
    d_sh = nc.dram_tensor("sin_hi", [P, TA], F32, kind="ExternalInput")
    d_sl = nc.dram_tensor("sin_lo", [P, TA], F32, kind="ExternalInput")
    d_id = nc.dram_tensor("ident", [P, P], F32, kind="ExternalInput")
    d_on = nc.dram_tensor("ones1", [1, P], F32, kind="ExternalInput")
    d_mk = nc.dram_tensor("mask", [P, P], F32, kind="ExternalInput")
    d_out = nc.dram_tensor("out", [1024, C], F32, kind="ExternalOutput")
    d_sqkv = nc.dram_tensor("sqkv", [NOC * P, TA], F32)
    d_sy = nc.dram_tensor("sy", [HPC * P, T], F32)
    d_psnd = [nc.dram_tensor(f"psend{i}", [2, 512, C], F32) for i in range(2)]
    d_prcv = [nc.dram_tensor(f"precv{i}", [512, C], F32) for i in range(2)]
    d_sx2 = nc.dram_tensor("sx2", [C, 1024], F32)

    sb_id = nc.alloc_sbuf_tensor("sb_id", [P, P], F32)
    sb_on = nc.alloc_sbuf_tensor("sb_on", [1, P], F32)
    sb_mk = nc.alloc_sbuf_tensor("sb_mk", [P, P], F32)
    sb_gat = nc.alloc_sbuf_tensor("sb_gat", [1, 1], F32)
    sb_gb = nc.alloc_sbuf_tensor("sb_gb", [P, 1], F32)
    sb_bb = nc.alloc_sbuf_tensor("sb_bb", [1, C], F32)
    sb_war = nc.alloc_sbuf_tensor("sb_war", [P, NCC, 24], F32)
    sb_wBT = nc.alloc_sbuf_tensor("sb_wBT", [16, C], F32)
    sb_cor = nc.alloc_sbuf_tensor("sb_cor", [16, P], F32)

    def dma(eng, sem, dst, src, waits=(), accum=None):
        if accum is None:
            rec = ir.add(eng, lambda e, d=dst, s=src: e.dma_start(out=d, in_=s), waits)
        else:
            rec = ir.add(eng,
                         lambda e, d=dst, s=src, a=accum: e.dma_start(out=d, in_=s,
                                                                      accum_op=a),
                         waits)
        return _IR.inc(rec, sem, 16)

    dma("sync", LDC, sb_id[:, :], d_id[:, :])
    dma("sync", LDC, sb_on[:, :], d_on[:, :])
    dma("sync", LDC, sb_mk[:, :], d_mk[:, :])
    dma("sync", LDC, sb_gat[:, :], d_gat[:, :])
    dma("sync", LDC, sb_bb[:, :], d_bb[:, :])
    dma("sync", LDC, sb_war[:, :, :], d_war[:, :, :])
    dma("sync", LDC, sb_wBT[:, :], d_wBT[:, :])
    dma("sync", LDC, sb_cor[:, :], d_cor[:, :])

    es1 = contextlib.ExitStack()
    xf = es1.enter_context(nc.sbuf_tensor("xf", [P, NCC, 522], F32))
    wsb = [es1.enter_context(nc.sbuf_tensor(f"w1_{i}", [P, NCC, P], F32)) for i in range(2)]
    ev = [es1.enter_context(nc.sbuf_tensor(f"ev{i}", [P, 522], F32)) for i in range(2)]
    sw = [es1.enter_context(nc.sbuf_tensor(f"sw{i}", [P, 522], F32)) for i in range(2)]
    rp = [es1.enter_context(nc.sbuf_tensor(f"rp{i}", [P, 522], F32)) for i in range(2)]
    tmp1 = es1.enter_context(nc.sbuf_tensor("tmp1", [P, 522], F32))
    cosf = es1.enter_context(nc.sbuf_tensor("cosf", [P, TA], F32))
    sinf = es1.enter_context(nc.sbuf_tensor("sinf", [P, TA], F32))
    clo = es1.enter_context(nc.sbuf_tensor("clo", [P, TA], F32))
    ps1 = [es1.enter_context(nc.psum_tensor(f"ps1_{i}", [P, 1024], F32)) for i in range(2)]

    dma("sync", LDC, cosf[:, :], d_ch[:, :])
    dma("sync", LDC, clo[:, :], d_cl[:, :])
    t_tab1 = dma("sync", LDC, sinf[:, :], d_sh[:, :])
    rec = ir.add("vector", lambda e: e.tensor_add(cosf[:, :], cosf[:, :], clo[:, :]),
                 waits=[(LDC.h, t_tab1)])
    t_cos = _IR.inc(rec, VC)
    t_tab2 = dma("sync", LDC, clo[:, :], d_sl[:, :], waits=[(VC.h, t_cos)])
    rec = ir.add("vector", lambda e: e.tensor_add(sinf[:, :], sinf[:, :], clo[:, :]),
                 waits=[(LDC.h, t_tab2)])
    _IR.inc(rec, VC)
    t_const = t_tab1

    with nc.psum_tensor("ps0", [P, 512], F32) as ps0:
        rec = ir.add("tensor",
                     lambda e: e.matmul(ps0[:, 0:1], sb_on[0:1, :], sb_gat[0:1, 0:1],
                                        start=True, stop=True),
                     waits=[(LDC.h, t_const)])
        t_mm0 = _IR.inc(rec, MM)
        rec = ir.add("scalar", lambda e: e.copy(sb_gb[:, 0:1], ps0[:, 0:1]),
                     waits=[(MM.h, t_mm0)])
        t_sc0 = _IR.inc(rec, SC)
    t_ph0_sc = t_sc0

    def _stop_early(level, waits):
        if trunc <= level and not ir.stopped:
            rec = ir.add("gpsimd",
                         lambda e: e.dma_start(out=d_out[:, :], in_=d_xtm[:, :]),
                         waits)
            _IR.inc(rec, OUT, 16)
            ir.stopped = True

    w_free = [None, None]
    ps_free = [t_ph0_sc, t_ph0_sc]
    ev_free = [[], []]
    sw_free = [[], []]
    rp_free = [[], []]
    last_mm_tile = None
    for tt in range(4):
        W = TILE_W[tt]
        t0 = 512 * tt
        xw = [(MM.h, last_mm_tile)] if last_mm_tile is not None else []
        t_x = dma("sync", LDX, xf[:, 0:16, 0:W], d_xfm[:, 0:16, t0:t0 + W], waits=xw)
        t_x2 = dma("sync", LDX2, xf[:, 16:NCC, 0:W], d_xfm[:, 16:NCC, t0:t0 + W])
        for oc in range(NOC):
            sl = oc % 2
            t_w = dma("gpsimd", LDW[sl], wsb[sl][:, :, :], d_wqkv[oc, :, :, :],
                      waits=([(MM.h, w_free[sl])] if w_free[sl] is not None else []))
            first = [(LDW[sl].h, t_w), (LDX.h, t_x), (SC.h, ps_free[sl])]
            rec = None
            for cc in range(NCC):
                cw = first if cc == 0 else (
                    [(LDX2.h, t_x2)] if (cc == 16 and oc == 0) else [])
                rec = ir.add("tensor",
                             lambda e, sl=sl, cc=cc: e.matmul(
                                 ps1[sl][:, 0:512], wsb[sl][:, cc, :], xf[:, cc, 0:512],
                                 start=(cc == 0), stop=(cc == NCC - 1)),
                             waits=cw)
            if W > 512:
                for cc in range(NCC):
                    rec = ir.add("tensor",
                                 lambda e, sl=sl, cc=cc, W=W: e.matmul(
                                     ps1[sl][:, 512:W], wsb[sl][:, cc, :], xf[:, cc, 512:W],
                                     start=(cc == 0), stop=(cc == NCC - 1)))
            t_mm = _IR.inc(rec, MM)
            w_free[sl] = t_mm
            rec = ir.add("scalar",
                         lambda e, sl=sl, W=W: e.copy(ev[sl][:, 0:W], ps1[sl][:, 0:W]),
                         waits=[(MM.h, t_mm)] + ev_free[sl])
            t_ev = _IR.inc(rec, SC)
            ps_free[sl] = t_ev
            if oc < 2 * HPC:
                rec = ir.add("gpsimd",
                             lambda e, sl=sl, W=W: e.dma_start(out=sw[sl][0:64, 0:W],
                                                               in_=ev[sl][64:128, 0:W]),
                             waits=[(SC.h, t_ev)] + sw_free[sl])
                _IR.inc(rec, SW[sl], 16)
                rec = ir.add("gpsimd",
                             lambda e, sl=sl, W=W: e.dma_start(out=sw[sl][64:128, 0:W],
                                                               in_=ev[sl][0:64, 0:W]))
                t_swp = _IR.inc(rec, SW[sl], 16)
                ir.add("vector",
                       lambda e, sl=sl, W=W, t0=t0: e.tensor_mul(
                           tmp1[:, 0:W], sw[sl][:, 0:W], sinf[:, t0:t0 + W]),
                       waits=[(SW[sl].h, t_swp)] + rp_free[sl])
                ir.add("vector",
                       lambda e, sl=sl, W=W, t0=t0: e.tensor_mul(
                           rp[sl][:, 0:W], ev[sl][:, 0:W], cosf[:, t0:t0 + W]))
                rec = ir.add("vector",
                             lambda e, sl=sl, W=W: e.tensor_add(
                                 rp[sl][:, 0:W], rp[sl][:, 0:W], tmp1[:, 0:W]))
                t_rope = _IR.inc(rec, VC)
                sp_prev = (SP[sl].h, SP[sl].v) if SP[sl].v else None
                t_sp = dma("gpsimd", SP[sl], d_sqkv[oc * P:(oc + 1) * P, t0:t0 + W],
                           rp[sl][:, 0:W], waits=[(VC.h, t_rope), sp_prev])
                ev_free[sl] = [(SW[sl].h, t_swp), (VC.h, t_rope)]
                sw_free[sl] = [(VC.h, t_rope)]
                rp_free[sl] = [(SP[sl].h, t_sp)]
            else:
                sp_prev = (SP[sl].h, SP[sl].v) if SP[sl].v else None
                t_sp = dma("gpsimd", SP[sl], d_sqkv[oc * P:(oc + 1) * P, t0:t0 + W],
                           ev[sl][:, 0:W], waits=[(SC.h, t_ev), sp_prev])
                ev_free[sl] = [(SP[sl].h, t_sp)]
        last_mm_tile = t_mm
    t_p1_done = [(SP[0].h, SP[0].v), (SP[1].h, SP[1].v)]
    es1.close()
    _stop_early(1, list(t_p1_done))

    es2 = contextlib.ExitStack()
    qf = [es2.enter_context(nc.sbuf_tensor(f"qf{i}", [P, T], F32)) for i in range(2)]
    kf = [es2.enter_context(nc.sbuf_tensor(f"kf{i}", [P, TA], F32)) for i in range(2)]
    vf = [es2.enter_context(nc.sbuf_tensor(f"vf{i}", [P, TA], F32)) for i in range(2)]
    vt = es2.enter_context(nc.sbuf_tensor("vt", [P, 17 * P], F32))
    prb = es2.enter_context(nc.sbuf_tensor("prb", [P, T], F32))
    pra = es2.enter_context(nc.sbuf_tensor("pra", [P, 16], F32))
    ptb = [es2.enter_context(nc.sbuf_tensor(f"ptb{i}", [P, P], F32)) for i in range(2)]
    pta = es2.enter_context(nc.sbuf_tensor("pta", [16, P], F32))
    yst = [es2.enter_context(nc.sbuf_tensor(f"yst{i}", [P, T], F32)) for i in range(2)]
    sml = es2.enter_context(nc.sbuf_tensor("sml", [P, 8], F32))
    pss = es2.enter_context(nc.psum_tensor("pss", [P, 2048], F32))
    psa = es2.enter_context(nc.psum_tensor("psa", [P, 512], F32))
    psy = es2.enter_context(nc.psum_tensor("psy", [P, 512], F32))
    ptp = [es2.enter_context(nc.psum_tensor(f"ptp{i}", [P, 512], F32)) for i in range(2)]

    qkv_free = [None, None]
    ptp_free = [[], []]
    yst_free = [None, None]
    for h in range(HPC):
        sl = h % 2
        lw = list(t_p1_done) + ([(MM.h, qkv_free[sl])] if qkv_free[sl] is not None else [])
        t_q = dma("sync", LQ[sl], qf[sl][:, :], d_sqkv[h * P:(h + 1) * P, 0:T], waits=lw)
        t_k = dma("sync", LK[sl], kf[sl][:, :], d_sqkv[(HPC + h) * P:(HPC + h + 1) * P, :])
        t_v = dma("sync", LV[sl], vf[sl][:, :],
                  d_sqkv[(2 * HPC + h) * P:(2 * HPC + h + 1) * P, :])
        def emit_vtr(j):
            bw = P if j < 16 else AT
            tsl = j % 2
            rec = ir.add("tensor",
                         lambda e, sl=sl, j=j, bw=bw, tsl=tsl: e.transpose(
                             ptp[tsl][0:bw, 0:P], vf[sl][:, j * P:j * P + bw], sb_id[:, :]),
                         waits=(([(LV[sl].h, t_v), (LDC.h, t_const)] if j == 0 else []) +
                                ptp_free[tsl]))
            return _IR.inc(rec, MM)

        def emit_vev(j, t_tr):
            bw = P if j < 16 else AT
            tsl = j % 2
            if j % 2 == 0:
                rec = ir.add("scalar",
                             lambda e, j=j, bw=bw, tsl=tsl: e.copy(
                                 vt[0:bw, j * P:(j + 1) * P], ptp[tsl][0:bw, 0:P]),
                             waits=[(MM.h, t_tr)])
                ptp_free[tsl] = [(SC.h, _IR.inc(rec, SC))]
            else:
                rec = ir.add("vector",
                             lambda e, j=j, bw=bw, tsl=tsl: e.tensor_copy(
                                 vt[0:bw, j * P:(j + 1) * P], ptp[tsl][0:bw, 0:P]),
                             waits=[(MM.h, t_tr)])
                ptp_free[tsl] = [(VC.h, _IR.inc(rec, VC))]

        t_trv = [None] * 17
        for j in range(17):
            t_trv[j] = emit_vtr(j)
            if j >= 1:
                emit_vev(j - 1, t_trv[j - 1])
        emit_vev(16, t_trv[16])

        t_exp = [None] * NQB
        t_expa = [None] * NQB
        t_max = [None] * NQB
        t_amax = [None] * NQB
        t_pn = [None] * NQB
        t_lastT = [None] * NQB
        t_ymm = [None] * NQB
        t_yev = [None] * NQB
        ptb_free = [[], []]

        t_smm_arr = [None] * NQB

        def emit_scores_mm(i, pv):
            KL = (i + 1) * P
            first = [(LQ[sl].h, t_q), (LK[sl].h, t_k)] if pv is None else []
            if pv is not None:
                first += [(SC.h, t_exp[pv]), (VC.h, t_max[pv])]
            ng = (KL + 511) // 512
            for kg in range(ng):
                kw = min(512, KL - kg * 512)
                ir.add("tensor",
                       lambda e, sl=sl, i=i, kg=kg, kw=kw: e.matmul(
                           pss[:, kg * 512:kg * 512 + kw],
                           qf[sl][:, i * P:(i + 1) * P],
                           kf[sl][:, kg * 512:kg * 512 + kw],
                           start=True, stop=True),
                       waits=(first if kg == 0 else []))
            aw = [] if pv is None else [(SC.h, t_expa[pv]), (VC.h, t_amax[pv])]
            rec = ir.add("tensor",
                         lambda e, sl=sl, i=i: e.matmul(
                             psa[:, 0:AT], qf[sl][:, i * P:(i + 1) * P],
                             kf[sl][:, T:TA], start=True, stop=True),
                         waits=aw)
            t_smm_arr[i] = _IR.inc(rec, MM)

        def emit_softmax(i, pv):
            KL = (i + 1) * P
            t_smm = t_smm_arr[i]
            ir.add("vector",
                   lambda e, i=i: e.tensor_add(pss[:, i * P:(i + 1) * P],
                                               pss[:, i * P:(i + 1) * P], sb_mk[:, :]),
                   waits=[(MM.h, t_smm)])
            ir.add("vector",
                   lambda e, KL=KL: e.reduce_max(sml[:, 0:1], pss[:, 0:KL],
                                                 AX.X, negate=True))
            rec = ir.add("vector",
                         lambda e: e.tensor_scalar_mul(sml[:, 0:1], sml[:, 0:1], SC_QK))
            t_max[i] = _IR.inc(rec, VC)
            ew = [(VC.h, t_max[i])]
            if pv is not None:
                ew.append((MM.h, t_lastT[pv]))
            rec = ir.add("scalar",
                         lambda e, KL=KL: e.activation(prb[:, 0:KL], pss[:, 0:KL], AF.Exp,
                                                       bias=sml[:, 0:1], scale=SC_QK,
                                                       accum_out=sml[:, 1:2]),
                         waits=ew)
            t_exp[i] = _IR.inc(rec, SC)
            ir.add("vector",
                   lambda e: e.reduce_max(sml[:, 3:4], psa[:, 0:AT], AX.X, negate=True),
                   waits=[(MM.h, t_smm)])
            rec = ir.add("vector",
                         lambda e: e.tensor_scalar_mul(sml[:, 3:4], sml[:, 3:4], SC_AD))
            t_amax[i] = _IR.inc(rec, VC)
            rec = ir.add("scalar",
                         lambda e: e.activation(pra[:, 0:AT], psa[:, 0:AT], AF.Exp,
                                                bias=sml[:, 3:4], scale=SC_AD,
                                                accum_out=sml[:, 4:5]),
                         waits=[(VC.h, t_amax[i])])
            t_expa[i] = _IR.inc(rec, SC)
            ir.add("vector", lambda e: e.reciprocal(sml[:, 2:3], sml[:, 1:2]),
                   waits=[(SC.h, t_exp[i])])
            ir.add("vector",
                   lambda e, KL=KL: e.tensor_scalar(prb[:, 0:KL], prb[:, 0:KL],
                                                    sml[:, 2:3], None, ALU.mult))
            ir.add("vector", lambda e: e.reciprocal(sml[:, 5:6], sml[:, 4:5]),
                   waits=[(SC.h, t_expa[i])])
            ir.add("vector", lambda e: e.tensor_mul(sml[:, 6:7], sml[:, 5:6], sb_gb[:, 0:1]),
                   waits=[(SC.h, t_ph0_sc)])
            rec = ir.add("vector",
                         lambda e: e.tensor_scalar(pra[:, 0:AT], pra[:, 0:AT],
                                                   sml[:, 6:7], None, ALU.mult))
            t_pn[i] = _IR.inc(rec, VC)

        def emit_y(i, pv, first_qb):
            t_pe = [None] * (i + 1)

            def emit_T(kb):
                tsl = kb % 2
                rec = ir.add("tensor",
                             lambda e, kb=kb, tsl=tsl: e.transpose(
                                 ptp[tsl][:, 0:P], prb[:, kb * P:(kb + 1) * P],
                                 sb_id[:, :]),
                             waits=([(VC.h, t_pn[i])] if kb == 0 else []) + ptp_free[tsl])
                return _IR.inc(rec, MM)

            def emit_ev(kb, t_tr):
                tsl = kb % 2
                if kb % 2 == 0:
                    rec = ir.add("scalar",
                                 lambda e, tsl=tsl: e.copy(ptb[tsl][:, :],
                                                           ptp[tsl][:, 0:P]),
                                 waits=[(MM.h, t_tr)] + ptb_free[tsl])
                    t_pe[kb] = (SC.h, _IR.inc(rec, SC))
                else:
                    rec = ir.add("vector",
                                 lambda e, tsl=tsl: e.tensor_copy(ptb[tsl][:, :],
                                                                  ptp[tsl][:, 0:P]),
                                 waits=[(MM.h, t_tr)] + ptb_free[tsl])
                    t_pe[kb] = (VC.h, _IR.inc(rec, VC))
                ptp_free[tsl] = [t_pe[kb]]

            def emit_Y(kb):
                tsl = kb % 2
                yw = [t_pe[kb]]
                if kb == 0 and pv is not None and t_yev[pv] is not None:
                    yw.append((SC.h, t_yev[pv]))
                rec = ir.add("tensor",
                             lambda e, kb=kb, tsl=tsl: e.matmul(
                                 psy[:, 0:P], vt[:, kb * P:(kb + 1) * P], ptb[tsl][:, :],
                                 start=(kb == 0), stop=False),
                             waits=yw)
                ptb_free[tsl] = [(MM.h, _IR.inc(rec, MM))]

            t_trs = [None] * (i + 1)
            t_trs[0] = emit_T(0)
            for kb in range(1, i + 1):
                t_trs[kb] = emit_T(kb)
                emit_ev(kb - 1, t_trs[kb - 1])
                emit_Y(kb - 1)
            tsl = (i + 1) % 2
            rec = ir.add("tensor",
                         lambda e, tsl=tsl: e.transpose(ptp[tsl][0:AT, 0:P],
                                                        pra[:, 0:AT], sb_id[:, :]),
                         waits=ptp_free[tsl])
            t_tra = _IR.inc(rec, MM)
            t_lastT[i] = t_tra
            emit_ev(i, t_trs[i])
            emit_Y(i)
            rec = ir.add("scalar",
                         lambda e, tsl=tsl: e.copy(pta[0:AT, :], ptp[tsl][0:AT, 0:P]),
                         waits=[(MM.h, t_tra)])
            t_pae = _IR.inc(rec, SC)
            ptp_free[tsl] = [(SC.h, t_pae)]
            rec = ir.add("tensor",
                         lambda e: e.matmul(psy[:, 0:P], vt[0:AT, 16 * P:17 * P],
                                            pta[0:AT, :], start=False, stop=True),
                         waits=[(SC.h, t_pae)])
            t_ymm[i] = _IR.inc(rec, MM)
            yw = [(MM.h, t_ymm[i])]
            if h >= 2 and first_qb and yst_free[sl] is not None:
                yw.append((YS[sl].h, yst_free[sl]))
            rec = ir.add("scalar",
                         lambda e, sl=sl, i=i: e.copy(yst[sl][:, i * P:(i + 1) * P],
                                                      psy[:, 0:P]),
                         waits=yw)
            t_yev[i] = _IR.inc(rec, SC)

        order = list(range(NQB - 1, -1, -1))
        prev = None
        for idx in range(NQB + 1):
            if idx < NQB:
                emit_scores_mm(order[idx], prev)
            if idx >= 1:
                emit_y(order[idx - 1], order[idx - 2] if idx >= 2 else None,
                       first_qb=(idx == 1))
            if idx < NQB:
                emit_softmax(order[idx], prev)
                prev = order[idx]
        t_ysp = dma("gpsimd", YS[sl], d_sy[h * P:(h + 1) * P, :], yst[sl][:, :],
                    waits=[(SC.h, t_yev[order[-1]])])
        yst_free[sl] = t_ysp
        qkv_free[sl] = t_ymm[order[-1]]
    t_p2_done = [(YS[0].h, YS[0].v), (YS[1].h, YS[1].v)]
    t_p2_sc = SC.v
    t_p2_vc = VC.v
    es2.close()
    _stop_early(2, list(t_p2_done))

    es3 = contextlib.ExitStack()
    ysb = [es3.enter_context(nc.sbuf_tensor(f"y3_{i}", [P, 512], F32)) for i in range(2)]
    wpb = [es3.enter_context(nc.sbuf_tensor(f"wp3_{i}", [P, 2, 512], F32)) for i in range(2)]
    stg = [es3.enter_context(nc.sbuf_tensor(f"stg{i}", [P, 512], F32)) for i in range(4)]

    with contextlib.ExitStack() as esp:
        psp = [esp.enter_context(nc.psum_tensor(f"psp{i}", [P, 512], F32)) for i in range(8)]
        y_free = [[], []]
        stg_free = [[] for _ in range(4)]
        bank_free = [[(SC.h, t_p2_sc), (VC.h, t_p2_vc)] for _ in range(8)]
        t_cc = None
        for th in range(2):
            for sp in range(2):
                for og2 in range(4):
                    t_mm = None
                    for cc in range(16):
                        sl = cc % 2
                        t_y = dma("sync", LY[sl], ysb[sl][:, :],
                                  d_sy[cc * P:(cc + 1) * P,
                                       sp * 1024 + th * 512:sp * 1024 + th * 512 + 512],
                                  waits=list(t_p2_done) + y_free[sl])
                        t_w = None
                        for j in range(2):
                            t_w = dma("sync", LW3[sl], wpb[sl][:, j, :],
                                      d_wp[cc, 2 * og2 + j, :, :])
                        first = [(LY[sl].h, t_y), (LW3[sl].h, t_w)]
                        rec = None
                        for j in range(2):
                            for tc in range(4):
                                bw_ = (first if (j == 0 and tc == 0) else [])
                                if cc == 0:
                                    bw_ = bw_ + bank_free[j * 4 + tc]
                                rec = ir.add("tensor",
                                             lambda e, sl=sl, j=j, tc=tc, cc=cc: e.matmul(
                                                 psp[j * 4 + tc][:, :],
                                                 ysb[sl][:, tc * P:(tc + 1) * P],
                                                 wpb[sl][:, j, :],
                                                 start=(cc == 0), stop=(cc == 15)),
                                             waits=bw_)
                        t_mm = _IR.inc(rec, MM)
                        y_free[sl] = [(MM.h, t_mm)]
                    k = 0
                    for j in range(2):
                        for tc in range(4):
                            sl2 = k % 4
                            if k % 2 == 0:
                                rec = ir.add("scalar",
                                             lambda e, sl2=sl2, j=j, tc=tc: e.copy(
                                                 stg[sl2][:, :], psp[j * 4 + tc][:, :]),
                                             waits=[(MM.h, t_mm)] + stg_free[sl2])
                                t_e = (SC.h, _IR.inc(rec, SC))
                            else:
                                rec = ir.add("vector",
                                             lambda e, sl2=sl2, j=j, tc=tc: e.tensor_copy(
                                                 stg[sl2][:, :], psp[j * 4 + tc][:, :]),
                                             waits=[(MM.h, t_mm)] + stg_free[sl2])
                                t_e = (VC.h, _IR.inc(rec, VC))
                            bank_free[j * 4 + tc] = [t_e]
                            t_s = dma("sync", SG[sl2],
                                      d_psnd[th][sp, tc * P:(tc + 1) * P,
                                                 (2 * og2 + j) * 512:(2 * og2 + j + 1) * 512],
                                      stg[sl2][:, :], waits=[t_e])
                            stg_free[sl2] = [(SG[sl2].h, t_s)]
                            k += 1
            sgw = [(SG[i].h, SG[i].v) for i in range(4)]
            if th == 0:
                _stop_early(3, sgw)
            rec = ir.add("gpsimd",
                         lambda e, th=th: e.collective_compute(
                             "ReduceScatter", ALU.add, RG,
                             ins=[d_psnd[th][:, :, :]], outs=[d_prcv[th][:, :]]),
                         waits=sgw)
            t_cc = _IR.inc(rec, CCS)
        _stop_early(4, [(CCS.h, t_cc)])

    res = [es3.enter_context(nc.sbuf_tensor(f"res{i}", [P, C], F32)) for i in range(4)]
    trash = es3.enter_context(nc.sbuf_tensor("trash", [P, C], F32))
    sm3 = es3.enter_context(nc.sbuf_tensor("sm3", [P, 16], F32))
    nr3 = es3.enter_context(nc.sbuf_tensor("nr3", [P, 4], F32))
    rsm3 = es3.enter_context(nc.sbuf_tensor("rsm3", [P, 4], F32))
    prr = es3.enter_context(nc.sbuf_tensor("prr", [P, 32], F32))
    ar_sb = es3.enter_context(nc.sbuf_tensor("ar_sb", [P, 96], F32))
    mix = es3.enter_context(nc.sbuf_tensor("mix", [P, 64], F32))
    t16 = es3.enter_context(nc.sbuf_tensor("t16", [P, 16], F32))
    axf = es3.enter_context(nc.sbuf_tensor("axf", [16, 512], F32))
    mxf = es3.enter_context(nc.sbuf_tensor("mxf", [16, 512], F32))
    x2st = [es3.enter_context(nc.sbuf_tensor(f"x2st{i}", [P, 512], F32)) for i in range(2)]
    nm3 = es3.enter_context(nc.sbuf_tensor("nm3", [P, 4], F32))
    rs3 = es3.enter_context(nc.sbuf_tensor("rs3", [P, 4], F32))
    xts = [es3.enter_context(nc.sbuf_tensor(f"xts{i}", [P, C], F32)) for i in range(2)]

    def emit_ln(tc, gate, t_ln_out):
        ir.add("vector", lambda e, tc=tc: e.reduce_sum(sm3[:, 0:1], res[tc][:, :], AX.X),
               waits=[gate])
        rec = ir.add("scalar",
                     lambda e, tc=tc: e.activation(trash[:, :], res[tc][:, :], AF.Square,
                                                   accum_out=sm3[:, 4 + tc:5 + tc]),
                     waits=[gate])
        t_sq = _IR.inc(rec, SC)
        ir.add("vector",
               lambda e, tc=tc: e.tensor_scalar(nm3[:, tc:tc + 1], sm3[:, 0:1],
                                                -1.0 / C, None, ALU.mult))
        ir.add("vector",
               lambda e, tc=tc: e.tensor_mul(sm3[:, 1:2], nm3[:, tc:tc + 1],
                                             nm3[:, tc:tc + 1]))
        ir.add("vector",
               lambda e, tc=tc: e.tensor_scalar(sm3[:, 2:3], sm3[:, 4 + tc:5 + tc],
                                                1.0 / C, EPS, ALU.mult, ALU.add),
               waits=[(SC.h, t_sq)])
        rec = ir.add("vector",
                     lambda e: e.tensor_sub(sm3[:, 2:3], sm3[:, 2:3], sm3[:, 1:2]))
        t_ve = _IR.inc(rec, VC)
        rec = ir.add("scalar",
                     lambda e, tc=tc: e.activation(sm3[:, 8 + tc:9 + tc], sm3[:, 2:3],
                                                   AF.Sqrt),
                     waits=[(VC.h, t_ve)])
        t_sd = _IR.inc(rec, SC)
        ir.add("vector", lambda e, tc=tc: e.reciprocal(rs3[:, tc:tc + 1],
                                                       sm3[:, 8 + tc:9 + tc]),
               waits=[(SC.h, t_sd)])
        rec = ir.add("vector",
                     lambda e, tc=tc: e.tensor_scalar(res[tc][:, :], res[tc][:, :],
                                                      nm3[:, tc:tc + 1], rs3[:, tc:tc + 1],
                                                      ALU.add, ALU.mult))
        t_ln_out[tc] = _IR.inc(rec, VC)

    xts_free = [None, None]
    for hf in range(2):
        for tc in range(4):
            rw = [(CCS.h, hf + 1)]
            if hf == 1:
                rw.append((OUT.h, 64))
            dma("sync", LDP, res[tc][:, :],
                d_prcv[hf][tc * P:(tc + 1) * P, :], waits=rw)
        t_pl_all = LDP.v
        t_add = [None] * 4
        for tc in range(4):
            sl = tc % 2
            t_xt = dma("sync", LXT[sl], xts[sl][:, :],
                       d_xtm[hf * 512 + tc * P:hf * 512 + (tc + 1) * P, :],
                       waits=([xts_free[sl]] if xts_free[sl] is not None else []))
            rec = ir.add("vector",
                         lambda e, tc=tc, sl=sl: e.tensor_add(res[tc][:, :],
                                                              res[tc][:, :],
                                                              xts[sl][:, :]),
                         waits=[(LDP.h, t_pl_all), (LXT[sl].h, t_xt)])
            t_add[tc] = _IR.inc(rec, VC)
            xts_free[sl] = (VC.h, t_add[tc])
        if hf == 0:
            _stop_early(5, [(VC.h, t_add[3])])
        t_ln1 = [None] * 4
        for tc in range(4):
            emit_ln(tc, (VC.h, t_add[tc]), t_ln1)
        if hf == 0:
            _stop_early(6, [(VC.h, t_ln1[3])])

        with contextlib.ExitStack() as esb:
            pt2 = [esb.enter_context(nc.psum_tensor(f"pt2_{hf}_{i}", [P, 512], F32))
                   for i in range(2)]
            psar = esb.enter_context(nc.psum_tensor(f"psar{hf}", [P, 512], F32))
            pspe = esb.enter_context(nc.psum_tensor(f"pspe{hf}", [P, 512], F32))
            pt2_free = [[], []]
            x2_free = [[], []]
            t_sx = [None] * NCC
            t_ar = None
            for cc in range(NCC):
                sl = cc % 2
                rec = None
                for tc in range(4):
                    rec = ir.add("tensor",
                                 lambda e, sl=sl, cc=cc, tc=tc: e.transpose(
                                     pt2[sl][:, tc * P:(tc + 1) * P],
                                     res[tc][:, cc * P:(cc + 1) * P], sb_id[:, :]),
                                 waits=(([(VC.h, t_ln1[3])] + pt2_free[sl])
                                        if tc == 0 else []))
                t_tr = _IR.inc(rec, MM)
                rec = ir.add("scalar",
                             lambda e, sl=sl: e.copy(x2st[sl][:, :], pt2[sl][:, :]),
                             waits=[(MM.h, t_tr)] + x2_free[sl])
                t_x2 = _IR.inc(rec, SC)
                pt2_free[sl] = [(SC.h, t_x2)]
                t_sx[cc] = dma("gpsimd", SX[sl],
                               d_sx2[cc * P:(cc + 1) * P, hf * 512:hf * 512 + 512],
                               x2st[sl][:, :], waits=[(SC.h, t_x2)])
                rec = None
                for tc in range(4):
                    rec = ir.add("tensor",
                                 lambda e, sl=sl, cc=cc, tc=tc: e.matmul(
                                     psar[:, tc * P:tc * P + 24],
                                     x2st[sl][:, tc * P:(tc + 1) * P],
                                     sb_war[:, cc, :],
                                     start=(cc == 0 and tc == 0),
                                     stop=(cc == NCC - 1 and tc == 3)),
                                 waits=([(SC.h, t_x2)] if tc == 0 else []))
                t_ar = _IR.inc(rec, MM)
                x2_free[sl] = [(SX[sl].h, t_sx[cc]), (MM.h, t_ar)]
            t_arev = None
            for tc in range(4):
                rec = ir.add("scalar",
                             lambda e, tc=tc: e.copy(ar_sb[:, tc * 24:(tc + 1) * 24],
                                                     psar[:, tc * P:tc * P + 24]),
                             waits=[(MM.h, t_ar)])
                t_arev = _IR.inc(rec, SC)

            t_mix = [None] * 4
            t_mxf = None
            for tc in range(4):
                rec = ir.add("vector",
                             lambda e, tc=tc: e.reduce_max(
                                 nr3[:, tc:tc + 1], ar_sb[:, tc * 24 + 16:tc * 24 + 24],
                                 AX.X, negate=True),
                             waits=[(SC.h, t_arev)])
                t_nr = _IR.inc(rec, VC)
                rec = ir.add("scalar",
                             lambda e, tc=tc: e.activation(
                                 prr[:, tc * 8:(tc + 1) * 8],
                                 ar_sb[:, tc * 24 + 16:tc * 24 + 24], AF.Exp,
                                 bias=nr3[:, tc:tc + 1], scale=1.0,
                                 accum_out=rsm3[:, tc:tc + 1]),
                             waits=[(VC.h, t_nr)])
                t_rexp = _IR.inc(rec, SC)
                ir.add("vector", lambda e, tc=tc: e.reciprocal(sm3[:, 3:4],
                                                               rsm3[:, tc:tc + 1]),
                       waits=[(SC.h, t_rexp)])
                rec = ir.add("vector",
                             lambda e, tc=tc: e.tensor_scalar(
                                 prr[:, tc * 8:(tc + 1) * 8], prr[:, tc * 8:(tc + 1) * 8],
                                 sm3[:, 3:4], None, ALU.mult))
                t_prr = _IR.inc(rec, VC)
                sl = tc % 2
                rec = ir.add("tensor",
                             lambda e, tc=tc, sl=sl: e.transpose(
                                 pt2[sl][0:16, 0:P], ar_sb[:, tc * 24:tc * 24 + 16],
                                 sb_id[:, :]),
                             waits=[(SC.h, t_arev)] + pt2_free[sl])
                t_axT = _IR.inc(rec, MM)
                rec = ir.add("scalar",
                             lambda e, tc=tc, sl=sl: e.copy(axf[0:16, tc * P:(tc + 1) * P],
                                                            pt2[sl][0:16, 0:P]),
                             waits=[(MM.h, t_axT)])
                t_axe = _IR.inc(rec, SC)
                pt2_free[sl] = [(SC.h, t_axe)]
                pw = [(SC.h, t_axe)]
                if tc > 0:
                    pw.append((VC.h, t_mix[tc - 1]))
                rec = None
                for ex in range(8):
                    rec = ir.add("tensor",
                                 lambda e, tc=tc, ex=ex: e.matmul(
                                     pspe[:, ex * 16:(ex + 1) * 16],
                                     axf[0:16, tc * P:(tc + 1) * P],
                                     sb_cor[0:16, ex * 16:(ex + 1) * 16],
                                     start=(ex == 0), stop=(ex == 7)),
                                 waits=(pw if ex == 0 else []))
                t_pe = _IR.inc(rec, MM)
                ir.add("vector",
                       lambda e, tc=tc: e.tensor_scalar(
                           mix[:, tc * 16:(tc + 1) * 16], pspe[:, 0:16],
                           prr[:, tc * 8:tc * 8 + 1], 2.0, ALU.mult, ALU.mult),
                       waits=[(MM.h, t_pe), (VC.h, t_prr)])
                for ex in range(1, 8):
                    ir.add("vector",
                           lambda e, tc=tc, ex=ex: e.tensor_scalar(
                               t16[:, 0:16], pspe[:, ex * 16:(ex + 1) * 16],
                               prr[:, tc * 8 + ex:tc * 8 + ex + 1], 2.0,
                               ALU.mult, ALU.mult))
                    ir.add("vector",
                           lambda e, tc=tc: e.tensor_add(mix[:, tc * 16:(tc + 1) * 16],
                                                         mix[:, tc * 16:(tc + 1) * 16],
                                                         t16[:, 0:16]))
                rec = ir.ops["vector"][-1]
                t_mix[tc] = _IR.inc(rec, VC)
                rec = ir.add("tensor",
                             lambda e, tc=tc, sl=sl: e.transpose(
                                 pt2[sl][0:16, 0:P], mix[:, tc * 16:(tc + 1) * 16],
                                 sb_id[:, :]),
                             waits=[(VC.h, t_mix[tc])] + pt2_free[sl])
                t_mxT = _IR.inc(rec, MM)
                rec = ir.add("scalar",
                             lambda e, tc=tc, sl=sl: e.copy(mxf[0:16, tc * P:(tc + 1) * P],
                                                            pt2[sl][0:16, 0:P]),
                             waits=[(MM.h, t_mxT)])
                t_mxf = _IR.inc(rec, SC)
                pt2_free[sl] = [(SC.h, t_mxf)]
            t_mix_last = t_mix[3]
            if hf == 0:
                _stop_early(7, [(SC.h, t_mxf), (VC.h, t_mix_last)])

        with contextlib.ExitStack() as esb:
            psb = [esb.enter_context(nc.psum_tensor(f"psb{hf}_{i}", [P, 512], F32))
                   for i in range(8)]
            yb_free = [[], []]
            bfree = [[(SC.h, t_mxf), (VC.h, t_mix_last)] for _ in range(8)]
            t_fev = None
            for og2 in range(4):
                for j in range(2):
                    for tc in range(4):
                        ir.add("tensor",
                               lambda e, j=j, tc=tc, og2=og2: e.matmul(
                                   psb[j * 4 + tc][:, :], sb_on[0:1, :],
                                   sb_bb[0:1, (2 * og2 + j) * 512:(2 * og2 + j + 1) * 512],
                                   start=True, stop=False),
                               waits=bfree[j * 4 + tc])
                t_mm = None
                for cc in range(NCC):
                    sl = cc % 2
                    t_y = dma("sync", LY[sl], ysb[sl][:, :],
                              d_sx2[cc * P:(cc + 1) * P, hf * 512:hf * 512 + 512],
                              waits=[(SX[sl].h, t_sx[cc])] + yb_free[sl])
                    t_w = None
                    for j in range(2):
                        t_w = dma("sync", LW3[sl], wpb[sl][:, j, :],
                                  d_wb[cc, 2 * og2 + j, :, :])
                    first = [(LY[sl].h, t_y), (LW3[sl].h, t_w)]
                    rec = None
                    for j in range(2):
                        for tc in range(4):
                            rec = ir.add("tensor",
                                         lambda e, sl=sl, j=j, tc=tc, cc=cc: e.matmul(
                                             psb[j * 4 + tc][:, :],
                                             ysb[sl][:, tc * P:(tc + 1) * P],
                                             wpb[sl][:, j, :],
                                             start=False, stop=False),
                                         waits=(first if (j == 0 and tc == 0) else []))
                    t_mm = _IR.inc(rec, MM)
                    yb_free[sl] = [(MM.h, t_mm)]
                rec = None
                t_lora = [None] * 8
                for j in range(2):
                    for tc in range(4):
                        rec = ir.add("tensor",
                                     lambda e, j=j, tc=tc, og2=og2: e.matmul(
                                         psb[j * 4 + tc][:, :],
                                         mxf[0:16, tc * P:(tc + 1) * P],
                                         sb_wBT[0:16,
                                                (2 * og2 + j) * 512:(2 * og2 + j + 1) * 512],
                                         start=False, stop=True),
                                     waits=([(SC.h, t_mxf)]
                                            if (j == 0 and tc == 0) else []))
                        t_lora[j * 4 + tc] = _IR.inc(rec, MM)
                for j in range(2):
                    for tc in range(4):
                        og = 2 * og2 + j
                        rec = ir.add("vector",
                                     lambda e, j=j, tc=tc, og=og: e.tensor_add(
                                         res[tc][:, og * 512:(og + 1) * 512],
                                         res[tc][:, og * 512:(og + 1) * 512],
                                         psb[j * 4 + tc][:, :]),
                                     waits=[(MM.h, t_lora[j * 4 + tc])])
                        t_fev = _IR.inc(rec, VC)
                        bfree[j * 4 + tc] = [(VC.h, t_fev)]

        if hf == 0:
            _stop_early(8, [(VC.h, t_fev)])
        t_ln2 = [None] * 4
        for tc in range(4):
            emit_ln(tc, (VC.h, t_fev), t_ln2)
        for tc in range(4):
            dma("gpsimd", OUT, d_out[hf * 512 + tc * P:hf * 512 + (tc + 1) * P, :],
                res[tc][:, :], waits=[(VC.h, t_ln2[tc])])
    es3.close()

    ir.add("gpsimd", lambda e: e.engine_nop(), waits=[(OUT.h, OUT.v)])

    with nc.Block() as blk:
        for name in ENGS:
            _self = ir.auto.get(name)
            def runner(eng, _ops=tuple(ir.ops[name]), _self=_self):
                n = 0
                for fn, waits, incs in _ops:
                    for (hh, vv) in waits:
                        eng.wait_ge(hh, vv)
                    if _self is not None and n > 0:
                        eng.wait_ge(_self.h, n)
                    inst = fn(eng)
                    for (hh, n2) in incs:
                        inst.then_inc(hh, n2)
                    n += 1
            getattr(blk, name)(runner)
    return nc


def _tables():
    th = 1.0 / (10000.0 ** (np.arange(0, HD, 2, dtype=np.float64) / HD))
    idx = np.outer(np.arange(T, dtype=np.float64), th)
    cos2 = np.concatenate([np.cos(idx), np.cos(idx)], axis=1).T
    sin2 = np.concatenate([np.sin(idx), np.sin(idx)], axis=1).T
    sin2[:64, :] = -sin2[:64, :]
    cosf = np.ones((P, TA), dtype=np.float64)
    sinf = np.zeros((P, TA), dtype=np.float64)
    cosf[:, :T] = cos2
    sinf[:, :T] = sin2
    out = []
    for t64 in (cosf, sinf):
        hi = t64.astype(np.float32)
        lo = (t64 - hi.astype(np.float64)).astype(np.float32)
        out += [np.ascontiguousarray(hi), np.ascontiguousarray(lo)]
    return out


_NC_CACHE = [None]


def kernel(L0_attn_x, L0_attn_w_attn, L0_attn_w_proj, L0_attn_adapter_wte, L0_attn_gating,
           L0_ffn_W_base, L0_ffn_b_base, L0_ffn_A_w, L0_ffn_B_w, L0_ffn_router_w,
           L0_ffn_cores):
    f = np.float32
    x = np.ascontiguousarray(np.asarray(L0_attn_x, f))
    wat = np.asarray(L0_attn_w_attn, f)
    wpr = np.asarray(L0_attn_w_proj, f)
    adw = np.asarray(L0_attn_adapter_wte, f)
    gat = np.ascontiguousarray(np.asarray(L0_attn_gating, f).reshape(1, 1))
    wba = np.asarray(L0_ffn_W_base, f)
    bb = np.ascontiguousarray(np.asarray(L0_ffn_b_base, f).reshape(1, C))
    aw = np.asarray(L0_ffn_A_w, f)
    bw = np.asarray(L0_ffn_B_w, f)
    rw = np.asarray(L0_ffn_router_w, f)
    cor = np.asarray(L0_ffn_cores, f)

    adT = adw.T
    xfm = {}
    for b in range(B):
        xc = np.concatenate([x[b].T, adT], axis=1)
        xfm[b] = np.ascontiguousarray(
            xc.reshape(NCC, P, TA).transpose(1, 0, 2))
    wqkv = {}
    for s in range(2):
        rows = np.concatenate([wat[2048 * s:2048 * s + 2048],
                               wat[C + 2048 * s:C + 2048 * s + 2048],
                               wat[2 * C + 2048 * s:2 * C + 2048 * s + 2048]], axis=0)
        wqkv[s] = np.ascontiguousarray(
            rows.T.reshape(NCC, P, NOC, P).transpose(2, 1, 0, 3))
    wp4 = {}
    for s in range(2):
        wp4[s] = np.ascontiguousarray(
            wpr.T[2048 * s:2048 * s + 2048].reshape(16, P, 8, 512).transpose(0, 2, 1, 3))
    wb4 = np.ascontiguousarray(
        wba.T.reshape(NCC, P, 8, 512).transpose(0, 2, 1, 3))
    war = np.ascontiguousarray(
        np.concatenate([aw.T, rw.T], axis=1).reshape(NCC, P, 24).transpose(1, 0, 2))
    wBT = np.ascontiguousarray(bw.T)
    corr = np.ascontiguousarray(cor.transpose(1, 0, 2).reshape(16, P))
    ch, cl, sh, slo = _tables()
    ident = np.eye(P, dtype=f)
    ones1 = np.ones((1, P), dtype=f)
    mask = np.where(np.arange(P)[None, :] <= np.arange(P)[:, None], f(0.0), f(NEG)).astype(f)

    if _NC_CACHE[0] is None:
        _NC_CACHE[0] = _build()
    nc = _NC_CACHE[0]

    in_maps = []
    for rank in range(8):
        b, s = rank // 2, rank % 2
        in_maps.append({
            "xfm3": xfm[b],
            "xtm": np.ascontiguousarray(x[b, s * 1024:(s + 1) * 1024, :]),
            "wqkv4": wqkv[s],
            "wp4": wp4[s],
            "wb4": wb4,
            "war3": war,
            "wBT": wBT,
            "bb": bb,
            "coresr": corr,
            "gat": gat,
            "cos_hi": ch, "cos_lo": cl, "sin_hi": sh, "sin_lo": slo,
            "ident": ident, "ones1": ones1, "mask": mask,
        })
    resu = run_bass_kernel_spmd(nc, in_maps, core_ids=list(range(8)))
    out = np.empty((B, T, C), dtype=f)
    for rank in range(8):
        b, s = rank // 2, rank % 2
        out[b, s * 1024:(s + 1) * 1024, :] = resu.results[rank]["out"]
    return out


# version 07
# speedup vs baseline: 1.1784x; 1.1784x over previous
import math
import contextlib
import numpy as np

import concourse.bass as bass
import concourse.mybir as mybir
from concourse.bass_utils import run_bass_kernel_spmd

F32 = mybir.dt.float32
AF = mybir.ActivationFunctionType
ALU = mybir.AluOpType
AX = mybir.AxisListType

P = 128
B, T, C = 4, 2048, 4096
H, HD, HPC = 32, 128, 16
AT = 10
TA = T + AT
NCC = C // P
NQB = T // P
NOC = 3 * HPC
SC_QK = 1.0 / math.sqrt(HD)
SC_AD = 1.0 / math.sqrt(C)
EPS = 1e-5
NEG = -1.0e30
TILE_W = (512, 512, 512, 522)
RG = [[0, 1], [2, 3], [4, 5], [6, 7]]

ENGS = ("sync", "gpsimd", "scalar", "vector", "tensor")


class _Sem:
    __slots__ = ("h", "v")

    def __init__(self, nc, name):
        self.h = nc.alloc_semaphore(name=name)
        self.v = 0


class _IR:

    def __init__(self):
        self.ops = {e: [] for e in ENGS}
        self.auto = {}
        self.stopped = False

    def add(self, eng, fn, waits=()):
        if self.stopped:
            return [None, [], []]
        rec = [fn, [w for w in waits if w is not None and w[1] is not None], []]
        sem = self.auto.get(eng)
        if sem is not None:
            rec[2].append((sem.h, 1))
            sem.v += 1
        self.ops[eng].append(rec)
        return rec

    @staticmethod
    def inc(rec, sem, n=1):
        if rec[0] is None:
            return sem.v
        for (hh, _n) in rec[2]:
            if hh is sem.h:
                return sem.v
        rec[2].append((sem.h, n))
        sem.v += n
        return sem.v


class _Stop(Exception):
    pass


def _build(trunc=9):
    nc = bass.Bass("TRN2", target_bir_lowering=False)
    ir = _IR()
    MM = _Sem(nc, "MM")
    SC = _Sem(nc, "SC")
    VC = _Sem(nc, "VC")
    ir.auto = {"vector": VC, "scalar": SC}
    CCS = _Sem(nc, "CCS")
    LDC = _Sem(nc, "LDC")
    LDX = _Sem(nc, "LDX")
    LDW = [_Sem(nc, f"LDW{i}") for i in range(2)]
    LQ = [_Sem(nc, f"LQ{i}") for i in range(2)]
    LK = [_Sem(nc, f"LK{i}") for i in range(2)]
    LV = [_Sem(nc, f"LV{i}") for i in range(2)]
    LY = [_Sem(nc, f"LY{i}") for i in range(2)]
    LW3 = [_Sem(nc, f"LW3{i}") for i in range(2)]
    LDP = _Sem(nc, "LDP")
    LDX2 = _Sem(nc, "LDX2")
    SW = [_Sem(nc, f"SW{i}") for i in range(2)]
    SP = [_Sem(nc, f"SP{i}") for i in range(2)]
    YS = [_Sem(nc, f"YS{i}") for i in range(2)]
    SG = [_Sem(nc, f"SG{i}") for i in range(4)]
    LXT = [_Sem(nc, f"LXT{i}") for i in range(2)]
    SX = [_Sem(nc, f"SX{i}") for i in range(2)]
    OUT = _Sem(nc, "OUT")

    d_xfm = nc.dram_tensor("xfm3", [P, NCC, TA], F32, kind="ExternalInput")
    d_xtm = nc.dram_tensor("xtm", [1024, C], F32, kind="ExternalInput")
    d_wqkv = nc.dram_tensor("wqkv4", [NOC, P, NCC, P], F32, kind="ExternalInput")
    d_wp = nc.dram_tensor("wp4", [16, 8, P, 512], F32, kind="ExternalInput")
    d_wb = nc.dram_tensor("wb4", [NCC, 8, P, 512], F32, kind="ExternalInput")
    d_war = nc.dram_tensor("war3", [P, NCC, 24], F32, kind="ExternalInput")
    d_wBT = nc.dram_tensor("wBT", [16, C], F32, kind="ExternalInput")
    d_bb = nc.dram_tensor("bb", [1, C], F32, kind="ExternalInput")
    d_cor = nc.dram_tensor("coresr", [16, P], F32, kind="ExternalInput")
    d_gat = nc.dram_tensor("gat", [1, 1], F32, kind="ExternalInput")
    d_ch = nc.dram_tensor("cos_hi", [P, TA], F32, kind="ExternalInput")
    d_cl = nc.dram_tensor("cos_lo", [P, TA], F32, kind="ExternalInput")
    d_sh = nc.dram_tensor("sin_hi", [P, TA], F32, kind="ExternalInput")
    d_sl = nc.dram_tensor("sin_lo", [P, TA], F32, kind="ExternalInput")
    d_id = nc.dram_tensor("ident", [P, P], F32, kind="ExternalInput")
    d_on = nc.dram_tensor("ones1", [1, P], F32, kind="ExternalInput")
    d_mk = nc.dram_tensor("mask", [P, P], F32, kind="ExternalInput")
    d_out = nc.dram_tensor("out", [1024, C], F32, kind="ExternalOutput")
    d_sqkv = nc.dram_tensor("sqkv", [NOC * P, TA], F32)
    d_sy = nc.dram_tensor("sy", [HPC * P, T], F32)
    d_psnd = [nc.dram_tensor(f"psend{i}", [2, 512, C], F32) for i in range(2)]
    d_prcv = [nc.dram_tensor(f"precv{i}", [512, C], F32) for i in range(2)]
    d_sx2 = nc.dram_tensor("sx2", [C, 1024], F32)

    sb_id = nc.alloc_sbuf_tensor("sb_id", [P, P], F32)
    sb_on = nc.alloc_sbuf_tensor("sb_on", [1, P], F32)
    sb_mk = nc.alloc_sbuf_tensor("sb_mk", [P, P], F32)
    sb_gat = nc.alloc_sbuf_tensor("sb_gat", [1, 1], F32)
    sb_gb = nc.alloc_sbuf_tensor("sb_gb", [P, 1], F32)
    sb_bb = nc.alloc_sbuf_tensor("sb_bb", [1, C], F32)
    sb_war = nc.alloc_sbuf_tensor("sb_war", [P, NCC, 24], F32)
    sb_wBT = nc.alloc_sbuf_tensor("sb_wBT", [16, C], F32)
    sb_cor = nc.alloc_sbuf_tensor("sb_cor", [16, P], F32)

    def dma(eng, sem, dst, src, waits=(), accum=None):
        if accum is None:
            rec = ir.add(eng, lambda e, d=dst, s=src: e.dma_start(out=d, in_=s), waits)
        else:
            rec = ir.add(eng,
                         lambda e, d=dst, s=src, a=accum: e.dma_start(out=d, in_=s,
                                                                      accum_op=a),
                         waits)
        return _IR.inc(rec, sem, 16)

    dma("sync", LDC, sb_id[:, :], d_id[:, :])
    dma("sync", LDC, sb_on[:, :], d_on[:, :])
    dma("sync", LDC, sb_mk[:, :], d_mk[:, :])
    dma("sync", LDC, sb_gat[:, :], d_gat[:, :])
    dma("sync", LDC, sb_bb[:, :], d_bb[:, :])
    dma("sync", LDC, sb_war[:, :, :], d_war[:, :, :])
    dma("sync", LDC, sb_wBT[:, :], d_wBT[:, :])
    dma("sync", LDC, sb_cor[:, :], d_cor[:, :])

    es1 = contextlib.ExitStack()
    xf = es1.enter_context(nc.sbuf_tensor("xf", [P, NCC, 522], F32))
    wsb = [es1.enter_context(nc.sbuf_tensor(f"w1_{i}", [P, NCC, P], F32)) for i in range(2)]
    ev = [es1.enter_context(nc.sbuf_tensor(f"ev{i}", [P, 522], F32)) for i in range(2)]
    sw = [es1.enter_context(nc.sbuf_tensor(f"sw{i}", [P, 522], F32)) for i in range(2)]
    rp = [es1.enter_context(nc.sbuf_tensor(f"rp{i}", [P, 522], F32)) for i in range(2)]
    tmp1 = es1.enter_context(nc.sbuf_tensor("tmp1", [P, 522], F32))
    cosf = es1.enter_context(nc.sbuf_tensor("cosf", [P, TA], F32))
    sinf = es1.enter_context(nc.sbuf_tensor("sinf", [P, TA], F32))
    clo = es1.enter_context(nc.sbuf_tensor("clo", [P, TA], F32))
    ps1 = [es1.enter_context(nc.psum_tensor(f"ps1_{i}", [P, 1024], F32)) for i in range(2)]

    dma("sync", LDC, cosf[:, :], d_ch[:, :])
    dma("sync", LDC, clo[:, :], d_cl[:, :])
    t_tab1 = dma("sync", LDC, sinf[:, :], d_sh[:, :])
    rec = ir.add("vector", lambda e: e.tensor_add(cosf[:, :], cosf[:, :], clo[:, :]),
                 waits=[(LDC.h, t_tab1)])
    t_cos = _IR.inc(rec, VC)
    t_tab2 = dma("sync", LDC, clo[:, :], d_sl[:, :], waits=[(VC.h, t_cos)])
    rec = ir.add("vector", lambda e: e.tensor_add(sinf[:, :], sinf[:, :], clo[:, :]),
                 waits=[(LDC.h, t_tab2)])
    _IR.inc(rec, VC)
    t_const = t_tab1

    with nc.psum_tensor("ps0", [P, 512], F32) as ps0:
        rec = ir.add("tensor",
                     lambda e: e.matmul(ps0[:, 0:1], sb_on[0:1, :], sb_gat[0:1, 0:1],
                                        start=True, stop=True),
                     waits=[(LDC.h, t_const)])
        t_mm0 = _IR.inc(rec, MM)
        rec = ir.add("scalar", lambda e: e.copy(sb_gb[:, 0:1], ps0[:, 0:1]),
                     waits=[(MM.h, t_mm0)])
        t_sc0 = _IR.inc(rec, SC)
    t_ph0_sc = t_sc0

    def _stop_early(level, waits):
        if trunc <= level and not ir.stopped:
            rec = ir.add("gpsimd",
                         lambda e: e.dma_start(out=d_out[:, :], in_=d_xtm[:, :]),
                         waits)
            _IR.inc(rec, OUT, 16)
            ir.stopped = True

    w_free = [None, None]
    ps_free = [t_ph0_sc, t_ph0_sc]
    ev_free = [[], []]
    sw_free = [[], []]
    rp_free = [[], []]
    last_mm_tile = None
    for tt in range(4):
        W = TILE_W[tt]
        t0 = 512 * tt
        xw = [(MM.h, last_mm_tile)] if last_mm_tile is not None else []
        t_x = dma("sync", LDX, xf[:, 0:16, 0:W], d_xfm[:, 0:16, t0:t0 + W], waits=xw)
        t_x2 = dma("sync", LDX2, xf[:, 16:NCC, 0:W], d_xfm[:, 16:NCC, t0:t0 + W])
        for oc in range(NOC):
            sl = oc % 2
            t_w = dma("sync", LDW[sl], wsb[sl][:, :, :], d_wqkv[oc, :, :, :],
                      waits=([(MM.h, w_free[sl])] if w_free[sl] is not None else []))
            first = [(LDW[sl].h, t_w), (LDX.h, t_x), (SC.h, ps_free[sl])]
            rec = None
            for cc in range(NCC):
                cw = first if cc == 0 else (
                    [(LDX2.h, t_x2)] if (cc == 16 and oc == 0) else [])
                rec = ir.add("tensor",
                             lambda e, sl=sl, cc=cc: e.matmul(
                                 ps1[sl][:, 0:512], wsb[sl][:, cc, :], xf[:, cc, 0:512],
                                 start=(cc == 0), stop=(cc == NCC - 1)),
                             waits=cw)
            if W > 512:
                for cc in range(NCC):
                    rec = ir.add("tensor",
                                 lambda e, sl=sl, cc=cc, W=W: e.matmul(
                                     ps1[sl][:, 512:W], wsb[sl][:, cc, :], xf[:, cc, 512:W],
                                     start=(cc == 0), stop=(cc == NCC - 1)))
            t_mm = _IR.inc(rec, MM)
            w_free[sl] = t_mm
            rec = ir.add("scalar",
                         lambda e, sl=sl, W=W: e.copy(ev[sl][:, 0:W], ps1[sl][:, 0:W]),
                         waits=[(MM.h, t_mm)] + ev_free[sl])
            t_ev = _IR.inc(rec, SC)
            ps_free[sl] = t_ev
            if oc < 2 * HPC:
                rec = ir.add("gpsimd",
                             lambda e, sl=sl, W=W: e.dma_start(out=sw[sl][0:64, 0:W],
                                                               in_=ev[sl][64:128, 0:W]),
                             waits=[(SC.h, t_ev)] + sw_free[sl])
                _IR.inc(rec, SW[sl], 16)
                rec = ir.add("gpsimd",
                             lambda e, sl=sl, W=W: e.dma_start(out=sw[sl][64:128, 0:W],
                                                               in_=ev[sl][0:64, 0:W]))
                t_swp = _IR.inc(rec, SW[sl], 16)
                ir.add("vector",
                       lambda e, sl=sl, W=W, t0=t0: e.tensor_mul(
                           tmp1[:, 0:W], sw[sl][:, 0:W], sinf[:, t0:t0 + W]),
                       waits=[(SW[sl].h, t_swp)] + rp_free[sl])
                ir.add("vector",
                       lambda e, sl=sl, W=W, t0=t0: e.tensor_mul(
                           rp[sl][:, 0:W], ev[sl][:, 0:W], cosf[:, t0:t0 + W]))
                rec = ir.add("vector",
                             lambda e, sl=sl, W=W: e.tensor_add(
                                 rp[sl][:, 0:W], rp[sl][:, 0:W], tmp1[:, 0:W]))
                t_rope = _IR.inc(rec, VC)
                sp_prev = (SP[sl].h, SP[sl].v) if SP[sl].v else None
                t_sp = dma("gpsimd", SP[sl], d_sqkv[oc * P:(oc + 1) * P, t0:t0 + W],
                           rp[sl][:, 0:W], waits=[(VC.h, t_rope), sp_prev])
                ev_free[sl] = [(SW[sl].h, t_swp), (VC.h, t_rope)]
                sw_free[sl] = [(VC.h, t_rope)]
                rp_free[sl] = [(SP[sl].h, t_sp)]
            else:
                sp_prev = (SP[sl].h, SP[sl].v) if SP[sl].v else None
                t_sp = dma("gpsimd", SP[sl], d_sqkv[oc * P:(oc + 1) * P, t0:t0 + W],
                           ev[sl][:, 0:W], waits=[(SC.h, t_ev), sp_prev])
                ev_free[sl] = [(SP[sl].h, t_sp)]
        last_mm_tile = t_mm
    t_p1_done = [(SP[0].h, SP[0].v), (SP[1].h, SP[1].v)]
    es1.close()
    _stop_early(1, list(t_p1_done))

    es2 = contextlib.ExitStack()
    qf = [es2.enter_context(nc.sbuf_tensor(f"qf{i}", [P, T], F32)) for i in range(2)]
    kf = [es2.enter_context(nc.sbuf_tensor(f"kf{i}", [P, TA], F32)) for i in range(2)]
    vf = [es2.enter_context(nc.sbuf_tensor(f"vf{i}", [P, TA], F32)) for i in range(2)]
    vt = es2.enter_context(nc.sbuf_tensor("vt", [P, 17 * P], F32))
    prb = es2.enter_context(nc.sbuf_tensor("prb", [P, T], F32))
    pra = es2.enter_context(nc.sbuf_tensor("pra", [P, 16], F32))
    ptb = [es2.enter_context(nc.sbuf_tensor(f"ptb{i}", [P, P], F32)) for i in range(2)]
    pta = es2.enter_context(nc.sbuf_tensor("pta", [16, P], F32))
    yst = [es2.enter_context(nc.sbuf_tensor(f"yst{i}", [P, T], F32)) for i in range(2)]
    sml = es2.enter_context(nc.sbuf_tensor("sml", [P, 8], F32))
    pss = es2.enter_context(nc.psum_tensor("pss", [P, 2048], F32))
    psa = es2.enter_context(nc.psum_tensor("psa", [P, 512], F32))
    psy = es2.enter_context(nc.psum_tensor("psy", [P, 512], F32))
    ptp = [es2.enter_context(nc.psum_tensor(f"ptp{i}", [P, 512], F32)) for i in range(2)]

    qkv_free = [None, None]
    ptp_free = [[], []]
    yst_free = [None, None]
    for h in range(HPC):
        sl = h % 2
        lw = list(t_p1_done) + ([(MM.h, qkv_free[sl])] if qkv_free[sl] is not None else [])
        t_q = dma("sync", LQ[sl], qf[sl][:, :], d_sqkv[h * P:(h + 1) * P, 0:T], waits=lw)
        t_k = dma("sync", LK[sl], kf[sl][:, :], d_sqkv[(HPC + h) * P:(HPC + h + 1) * P, :])
        t_v = dma("sync", LV[sl], vf[sl][:, :],
                  d_sqkv[(2 * HPC + h) * P:(2 * HPC + h + 1) * P, :])
        def emit_vtr(j):
            bw = P if j < 16 else AT
            tsl = j % 2
            rec = ir.add("tensor",
                         lambda e, sl=sl, j=j, bw=bw, tsl=tsl: e.transpose(
                             ptp[tsl][0:bw, 0:P], vf[sl][:, j * P:j * P + bw], sb_id[:, :]),
                         waits=(([(LV[sl].h, t_v), (LDC.h, t_const)] if j == 0 else []) +
                                ptp_free[tsl]))
            return _IR.inc(rec, MM)

        def emit_vev(j, t_tr):
            bw = P if j < 16 else AT
            tsl = j % 2
            if j % 2 == 0:
                rec = ir.add("scalar",
                             lambda e, j=j, bw=bw, tsl=tsl: e.copy(
                                 vt[0:bw, j * P:(j + 1) * P], ptp[tsl][0:bw, 0:P]),
                             waits=[(MM.h, t_tr)])
                ptp_free[tsl] = [(SC.h, _IR.inc(rec, SC))]
            else:
                rec = ir.add("vector",
                             lambda e, j=j, bw=bw, tsl=tsl: e.tensor_copy(
                                 vt[0:bw, j * P:(j + 1) * P], ptp[tsl][0:bw, 0:P]),
                             waits=[(MM.h, t_tr)])
                ptp_free[tsl] = [(VC.h, _IR.inc(rec, VC))]

        t_trv = [None] * 17
        for j in range(17):
            t_trv[j] = emit_vtr(j)
            if j >= 1:
                emit_vev(j - 1, t_trv[j - 1])
        emit_vev(16, t_trv[16])

        t_exp = [None] * NQB
        t_expa = [None] * NQB
        t_max = [None] * NQB
        t_amax = [None] * NQB
        t_pn = [None] * NQB
        t_lastT = [None] * NQB
        t_ymm = [None] * NQB
        t_yev = [None] * NQB
        ptb_free = [[], []]

        t_smm_arr = [None] * NQB

        def emit_scores_mm(i, pv):
            KL = (i + 1) * P
            first = [(LQ[sl].h, t_q), (LK[sl].h, t_k)] if pv is None else []
            if pv is not None:
                first += [(SC.h, t_exp[pv]), (VC.h, t_max[pv])]
            ng = (KL + 511) // 512
            for kg in range(ng):
                kw = min(512, KL - kg * 512)
                ir.add("tensor",
                       lambda e, sl=sl, i=i, kg=kg, kw=kw: e.matmul(
                           pss[:, kg * 512:kg * 512 + kw],
                           qf[sl][:, i * P:(i + 1) * P],
                           kf[sl][:, kg * 512:kg * 512 + kw],
                           start=True, stop=True),
                       waits=(first if kg == 0 else []))
            aw = [] if pv is None else [(SC.h, t_expa[pv]), (VC.h, t_amax[pv])]
            rec = ir.add("tensor",
                         lambda e, sl=sl, i=i: e.matmul(
                             psa[:, 0:AT], qf[sl][:, i * P:(i + 1) * P],
                             kf[sl][:, T:TA], start=True, stop=True),
                         waits=aw)
            t_smm_arr[i] = _IR.inc(rec, MM)

        def emit_softmax(i, pv):
            KL = (i + 1) * P
            t_smm = t_smm_arr[i]
            ir.add("vector",
                   lambda e, i=i: e.tensor_add(pss[:, i * P:(i + 1) * P],
                                               pss[:, i * P:(i + 1) * P], sb_mk[:, :]),
                   waits=[(MM.h, t_smm)])
            ir.add("vector",
                   lambda e, KL=KL: e.reduce_max(sml[:, 0:1], pss[:, 0:KL],
                                                 AX.X, negate=True))
            rec = ir.add("vector",
                         lambda e: e.tensor_scalar_mul(sml[:, 0:1], sml[:, 0:1], SC_QK))
            t_max[i] = _IR.inc(rec, VC)
            ew = [(VC.h, t_max[i])]
            if pv is not None:
                ew.append((MM.h, t_lastT[pv]))
            rec = ir.add("scalar",
                         lambda e, KL=KL: e.activation(prb[:, 0:KL], pss[:, 0:KL], AF.Exp,
                                                       bias=sml[:, 0:1], scale=SC_QK,
                                                       accum_out=sml[:, 1:2]),
                         waits=ew)
            t_exp[i] = _IR.inc(rec, SC)
            ir.add("vector",
                   lambda e: e.reduce_max(sml[:, 3:4], psa[:, 0:AT], AX.X, negate=True),
                   waits=[(MM.h, t_smm)])
            rec = ir.add("vector",
                         lambda e: e.tensor_scalar_mul(sml[:, 3:4], sml[:, 3:4], SC_AD))
            t_amax[i] = _IR.inc(rec, VC)
            rec = ir.add("scalar",
                         lambda e: e.activation(pra[:, 0:AT], psa[:, 0:AT], AF.Exp,
                                                bias=sml[:, 3:4], scale=SC_AD,
                                                accum_out=sml[:, 4:5]),
                         waits=[(VC.h, t_amax[i])])
            t_expa[i] = _IR.inc(rec, SC)
            ir.add("vector", lambda e: e.reciprocal(sml[:, 2:3], sml[:, 1:2]),
                   waits=[(SC.h, t_exp[i])])
            ir.add("vector",
                   lambda e, KL=KL: e.tensor_scalar(prb[:, 0:KL], prb[:, 0:KL],
                                                    sml[:, 2:3], None, ALU.mult))
            ir.add("vector", lambda e: e.reciprocal(sml[:, 5:6], sml[:, 4:5]),
                   waits=[(SC.h, t_expa[i])])
            ir.add("vector", lambda e: e.tensor_mul(sml[:, 6:7], sml[:, 5:6], sb_gb[:, 0:1]),
                   waits=[(SC.h, t_ph0_sc)])
            rec = ir.add("vector",
                         lambda e: e.tensor_scalar(pra[:, 0:AT], pra[:, 0:AT],
                                                   sml[:, 6:7], None, ALU.mult))
            t_pn[i] = _IR.inc(rec, VC)

        def emit_y(i, pv, first_qb):
            t_pe = [None] * (i + 1)

            def emit_T(kb):
                tsl = kb % 2
                rec = ir.add("tensor",
                             lambda e, kb=kb, tsl=tsl: e.transpose(
                                 ptp[tsl][:, 0:P], prb[:, kb * P:(kb + 1) * P],
                                 sb_id[:, :]),
                             waits=([(VC.h, t_pn[i])] if kb == 0 else []) + ptp_free[tsl])
                return _IR.inc(rec, MM)

            def emit_ev(kb, t_tr):
                tsl = kb % 2
                if kb % 2 == 0:
                    rec = ir.add("scalar",
                                 lambda e, tsl=tsl: e.copy(ptb[tsl][:, :],
                                                           ptp[tsl][:, 0:P]),
                                 waits=[(MM.h, t_tr)] + ptb_free[tsl])
                    t_pe[kb] = (SC.h, _IR.inc(rec, SC))
                else:
                    rec = ir.add("vector",
                                 lambda e, tsl=tsl: e.tensor_copy(ptb[tsl][:, :],
                                                                  ptp[tsl][:, 0:P]),
                                 waits=[(MM.h, t_tr)] + ptb_free[tsl])
                    t_pe[kb] = (VC.h, _IR.inc(rec, VC))
                ptp_free[tsl] = [t_pe[kb]]

            def emit_Y(kb):
                tsl = kb % 2
                yw = [t_pe[kb]]
                if kb == 0 and pv is not None and t_yev[pv] is not None:
                    yw.append((SC.h, t_yev[pv]))
                rec = ir.add("tensor",
                             lambda e, kb=kb, tsl=tsl: e.matmul(
                                 psy[:, 0:P], vt[:, kb * P:(kb + 1) * P], ptb[tsl][:, :],
                                 start=(kb == 0), stop=False),
                             waits=yw)
                ptb_free[tsl] = [(MM.h, _IR.inc(rec, MM))]

            t_trs = [None] * (i + 1)
            t_trs[0] = emit_T(0)
            for kb in range(1, i + 1):
                t_trs[kb] = emit_T(kb)
                emit_ev(kb - 1, t_trs[kb - 1])
                emit_Y(kb - 1)
            tsl = (i + 1) % 2
            rec = ir.add("tensor",
                         lambda e, tsl=tsl: e.transpose(ptp[tsl][0:AT, 0:P],
                                                        pra[:, 0:AT], sb_id[:, :]),
                         waits=ptp_free[tsl])
            t_tra = _IR.inc(rec, MM)
            t_lastT[i] = t_tra
            emit_ev(i, t_trs[i])
            emit_Y(i)
            rec = ir.add("scalar",
                         lambda e, tsl=tsl: e.copy(pta[0:AT, :], ptp[tsl][0:AT, 0:P]),
                         waits=[(MM.h, t_tra)])
            t_pae = _IR.inc(rec, SC)
            ptp_free[tsl] = [(SC.h, t_pae)]
            rec = ir.add("tensor",
                         lambda e: e.matmul(psy[:, 0:P], vt[0:AT, 16 * P:17 * P],
                                            pta[0:AT, :], start=False, stop=True),
                         waits=[(SC.h, t_pae)])
            t_ymm[i] = _IR.inc(rec, MM)
            yw = [(MM.h, t_ymm[i])]
            if h >= 2 and first_qb and yst_free[sl] is not None:
                yw.append((YS[sl].h, yst_free[sl]))
            rec = ir.add("scalar",
                         lambda e, sl=sl, i=i: e.copy(yst[sl][:, i * P:(i + 1) * P],
                                                      psy[:, 0:P]),
                         waits=yw)
            t_yev[i] = _IR.inc(rec, SC)

        order = list(range(NQB - 1, -1, -1))
        prev = None
        for idx in range(NQB + 1):
            if idx < NQB:
                emit_scores_mm(order[idx], prev)
            if idx >= 1:
                emit_y(order[idx - 1], order[idx - 2] if idx >= 2 else None,
                       first_qb=(idx == 1))
            if idx < NQB:
                emit_softmax(order[idx], prev)
                prev = order[idx]
        t_ysp = dma("gpsimd", YS[sl], d_sy[h * P:(h + 1) * P, :], yst[sl][:, :],
                    waits=[(SC.h, t_yev[order[-1]])])
        yst_free[sl] = t_ysp
        qkv_free[sl] = t_ymm[order[-1]]
    t_p2_done = [(YS[0].h, YS[0].v), (YS[1].h, YS[1].v)]
    t_p2_sc = SC.v
    t_p2_vc = VC.v
    es2.close()
    _stop_early(2, list(t_p2_done))

    es3 = contextlib.ExitStack()
    ysb = [es3.enter_context(nc.sbuf_tensor(f"y3_{i}", [P, 512], F32)) for i in range(2)]
    wpb = [es3.enter_context(nc.sbuf_tensor(f"wp3_{i}", [P, 2, 512], F32)) for i in range(2)]
    stg = [es3.enter_context(nc.sbuf_tensor(f"stg{i}", [P, 512], F32)) for i in range(4)]

    with contextlib.ExitStack() as esp:
        psp = [esp.enter_context(nc.psum_tensor(f"psp{i}", [P, 512], F32)) for i in range(8)]
        y_free = [[], []]
        stg_free = [[] for _ in range(4)]
        bank_free = [[(SC.h, t_p2_sc), (VC.h, t_p2_vc)] for _ in range(8)]
        t_cc = None
        for th in range(2):
            for sp in range(2):
                for og2 in range(4):
                    t_mm = None
                    for cc in range(16):
                        sl = cc % 2
                        t_y = dma("sync", LY[sl], ysb[sl][:, :],
                                  d_sy[cc * P:(cc + 1) * P,
                                       sp * 1024 + th * 512:sp * 1024 + th * 512 + 512],
                                  waits=list(t_p2_done) + y_free[sl])
                        t_w = None
                        for j in range(2):
                            t_w = dma("sync", LW3[sl], wpb[sl][:, j, :],
                                      d_wp[cc, 2 * og2 + j, :, :])
                        first = [(LY[sl].h, t_y), (LW3[sl].h, t_w)]
                        rec = None
                        for j in range(2):
                            for tc in range(4):
                                bw_ = (first if (j == 0 and tc == 0) else [])
                                if cc == 0:
                                    bw_ = bw_ + bank_free[j * 4 + tc]
                                rec = ir.add("tensor",
                                             lambda e, sl=sl, j=j, tc=tc, cc=cc: e.matmul(
                                                 psp[j * 4 + tc][:, :],
                                                 ysb[sl][:, tc * P:(tc + 1) * P],
                                                 wpb[sl][:, j, :],
                                                 start=(cc == 0), stop=(cc == 15)),
                                             waits=bw_)
                        t_mm = _IR.inc(rec, MM)
                        y_free[sl] = [(MM.h, t_mm)]
                    k = 0
                    for j in range(2):
                        for tc in range(4):
                            sl2 = k % 4
                            if k % 2 == 0:
                                rec = ir.add("scalar",
                                             lambda e, sl2=sl2, j=j, tc=tc: e.copy(
                                                 stg[sl2][:, :], psp[j * 4 + tc][:, :]),
                                             waits=[(MM.h, t_mm)] + stg_free[sl2])
                                t_e = (SC.h, _IR.inc(rec, SC))
                            else:
                                rec = ir.add("vector",
                                             lambda e, sl2=sl2, j=j, tc=tc: e.tensor_copy(
                                                 stg[sl2][:, :], psp[j * 4 + tc][:, :]),
                                             waits=[(MM.h, t_mm)] + stg_free[sl2])
                                t_e = (VC.h, _IR.inc(rec, VC))
                            bank_free[j * 4 + tc] = [t_e]
                            t_s = dma("sync", SG[sl2],
                                      d_psnd[th][sp, tc * P:(tc + 1) * P,
                                                 (2 * og2 + j) * 512:(2 * og2 + j + 1) * 512],
                                      stg[sl2][:, :], waits=[t_e])
                            stg_free[sl2] = [(SG[sl2].h, t_s)]
                            k += 1
            sgw = [(SG[i].h, SG[i].v) for i in range(4)]
            if th == 0:
                _stop_early(3, sgw)
            rec = ir.add("gpsimd",
                         lambda e, th=th: e.collective_compute(
                             "ReduceScatter", ALU.add, RG,
                             ins=[d_psnd[th][:, :, :]], outs=[d_prcv[th][:, :]]),
                         waits=sgw)
            t_cc = _IR.inc(rec, CCS)
        _stop_early(4, [(CCS.h, t_cc)])

    res = [es3.enter_context(nc.sbuf_tensor(f"res{i}", [P, C], F32)) for i in range(4)]
    trash = es3.enter_context(nc.sbuf_tensor("trash", [P, C], F32))
    sm3 = es3.enter_context(nc.sbuf_tensor("sm3", [P, 16], F32))
    nr3 = es3.enter_context(nc.sbuf_tensor("nr3", [P, 4], F32))
    rsm3 = es3.enter_context(nc.sbuf_tensor("rsm3", [P, 4], F32))
    prr = es3.enter_context(nc.sbuf_tensor("prr", [P, 32], F32))
    ar_sb = es3.enter_context(nc.sbuf_tensor("ar_sb", [P, 96], F32))
    mix = es3.enter_context(nc.sbuf_tensor("mix", [P, 64], F32))
    t16 = es3.enter_context(nc.sbuf_tensor("t16", [P, 16], F32))
    axf = es3.enter_context(nc.sbuf_tensor("axf", [16, 512], F32))
    mxf = es3.enter_context(nc.sbuf_tensor("mxf", [16, 512], F32))
    x2st = [es3.enter_context(nc.sbuf_tensor(f"x2st{i}", [P, 512], F32)) for i in range(2)]
    nm3 = es3.enter_context(nc.sbuf_tensor("nm3", [P, 4], F32))
    rs3 = es3.enter_context(nc.sbuf_tensor("rs3", [P, 4], F32))
    xts = [es3.enter_context(nc.sbuf_tensor(f"xts{i}", [P, C], F32)) for i in range(2)]

    def emit_ln(tc, gate, t_ln_out):
        ir.add("vector", lambda e, tc=tc: e.reduce_sum(sm3[:, 0:1], res[tc][:, :], AX.X),
               waits=[gate])
        rec = ir.add("scalar",
                     lambda e, tc=tc: e.activation(trash[:, :], res[tc][:, :], AF.Square,
                                                   accum_out=sm3[:, 4 + tc:5 + tc]),
                     waits=[gate])
        t_sq = _IR.inc(rec, SC)
        ir.add("vector",
               lambda e, tc=tc: e.tensor_scalar(nm3[:, tc:tc + 1], sm3[:, 0:1],
                                                -1.0 / C, None, ALU.mult))
        ir.add("vector",
               lambda e, tc=tc: e.tensor_mul(sm3[:, 1:2], nm3[:, tc:tc + 1],
                                             nm3[:, tc:tc + 1]))
        ir.add("vector",
               lambda e, tc=tc: e.tensor_scalar(sm3[:, 2:3], sm3[:, 4 + tc:5 + tc],
                                                1.0 / C, EPS, ALU.mult, ALU.add),
               waits=[(SC.h, t_sq)])
        rec = ir.add("vector",
                     lambda e: e.tensor_sub(sm3[:, 2:3], sm3[:, 2:3], sm3[:, 1:2]))
        t_ve = _IR.inc(rec, VC)
        rec = ir.add("scalar",
                     lambda e, tc=tc: e.activation(sm3[:, 8 + tc:9 + tc], sm3[:, 2:3],
                                                   AF.Sqrt),
                     waits=[(VC.h, t_ve)])
        t_sd = _IR.inc(rec, SC)
        ir.add("vector", lambda e, tc=tc: e.reciprocal(rs3[:, tc:tc + 1],
                                                       sm3[:, 8 + tc:9 + tc]),
               waits=[(SC.h, t_sd)])
        rec = ir.add("vector",
                     lambda e, tc=tc: e.tensor_scalar(res[tc][:, :], res[tc][:, :],
                                                      nm3[:, tc:tc + 1], rs3[:, tc:tc + 1],
                                                      ALU.add, ALU.mult))
        t_ln_out[tc] = _IR.inc(rec, VC)

    xts_free = [None, None]
    for hf in range(2):
        for tc in range(4):
            rw = [(CCS.h, hf + 1)]
            if hf == 1:
                rw.append((OUT.h, 64))
            dma("sync", LDP, res[tc][:, :],
                d_prcv[hf][tc * P:(tc + 1) * P, :], waits=rw)
        t_pl_all = LDP.v
        t_add = [None] * 4
        for tc in range(4):
            sl = tc % 2
            t_xt = dma("sync", LXT[sl], xts[sl][:, :],
                       d_xtm[hf * 512 + tc * P:hf * 512 + (tc + 1) * P, :],
                       waits=([xts_free[sl]] if xts_free[sl] is not None else []))
            rec = ir.add("vector",
                         lambda e, tc=tc, sl=sl: e.tensor_add(res[tc][:, :],
                                                              res[tc][:, :],
                                                              xts[sl][:, :]),
                         waits=[(LDP.h, t_pl_all), (LXT[sl].h, t_xt)])
            t_add[tc] = _IR.inc(rec, VC)
            xts_free[sl] = (VC.h, t_add[tc])
        if hf == 0:
            _stop_early(5, [(VC.h, t_add[3])])
        t_ln1 = [None] * 4
        for tc in range(4):
            emit_ln(tc, (VC.h, t_add[tc]), t_ln1)
        if hf == 0:
            _stop_early(6, [(VC.h, t_ln1[3])])

        with contextlib.ExitStack() as esb:
            pt2 = [esb.enter_context(nc.psum_tensor(f"pt2_{hf}_{i}", [P, 512], F32))
                   for i in range(2)]
            psar = esb.enter_context(nc.psum_tensor(f"psar{hf}", [P, 512], F32))
            pspe = esb.enter_context(nc.psum_tensor(f"pspe{hf}", [P, 512], F32))
            pt2_free = [[], []]
            x2_free = [[], []]
            t_sx = [None] * NCC
            t_ar = None
            for cc in range(NCC):
                sl = cc % 2
                rec = None
                for tc in range(4):
                    rec = ir.add("tensor",
                                 lambda e, sl=sl, cc=cc, tc=tc: e.transpose(
                                     pt2[sl][:, tc * P:(tc + 1) * P],
                                     res[tc][:, cc * P:(cc + 1) * P], sb_id[:, :]),
                                 waits=(([(VC.h, t_ln1[3])] + pt2_free[sl])
                                        if tc == 0 else []))
                t_tr = _IR.inc(rec, MM)
                rec = ir.add("scalar",
                             lambda e, sl=sl: e.copy(x2st[sl][:, :], pt2[sl][:, :]),
                             waits=[(MM.h, t_tr)] + x2_free[sl])
                t_x2 = _IR.inc(rec, SC)
                pt2_free[sl] = [(SC.h, t_x2)]
                t_sx[cc] = dma("gpsimd", SX[sl],
                               d_sx2[cc * P:(cc + 1) * P, hf * 512:hf * 512 + 512],
                               x2st[sl][:, :], waits=[(SC.h, t_x2)])
                rec = None
                for tc in range(4):
                    rec = ir.add("tensor",
                                 lambda e, sl=sl, cc=cc, tc=tc: e.matmul(
                                     psar[:, tc * P:tc * P + 24],
                                     x2st[sl][:, tc * P:(tc + 1) * P],
                                     sb_war[:, cc, :],
                                     start=(cc == 0 and tc == 0),
                                     stop=(cc == NCC - 1 and tc == 3)),
                                 waits=([(SC.h, t_x2)] if tc == 0 else []))
                t_ar = _IR.inc(rec, MM)
                x2_free[sl] = [(SX[sl].h, t_sx[cc]), (MM.h, t_ar)]
            t_arev = None
            for tc in range(4):
                rec = ir.add("scalar",
                             lambda e, tc=tc: e.copy(ar_sb[:, tc * 24:(tc + 1) * 24],
                                                     psar[:, tc * P:tc * P + 24]),
                             waits=[(MM.h, t_ar)])
                t_arev = _IR.inc(rec, SC)

            t_mix = [None] * 4
            t_mxf = None
            for tc in range(4):
                rec = ir.add("vector",
                             lambda e, tc=tc: e.reduce_max(
                                 nr3[:, tc:tc + 1], ar_sb[:, tc * 24 + 16:tc * 24 + 24],
                                 AX.X, negate=True),
                             waits=[(SC.h, t_arev)])
                t_nr = _IR.inc(rec, VC)
                rec = ir.add("scalar",
                             lambda e, tc=tc: e.activation(
                                 prr[:, tc * 8:(tc + 1) * 8],
                                 ar_sb[:, tc * 24 + 16:tc * 24 + 24], AF.Exp,
                                 bias=nr3[:, tc:tc + 1], scale=1.0,
                                 accum_out=rsm3[:, tc:tc + 1]),
                             waits=[(VC.h, t_nr)])
                t_rexp = _IR.inc(rec, SC)
                ir.add("vector", lambda e, tc=tc: e.reciprocal(sm3[:, 3:4],
                                                               rsm3[:, tc:tc + 1]),
                       waits=[(SC.h, t_rexp)])
                rec = ir.add("vector",
                             lambda e, tc=tc: e.tensor_scalar(
                                 prr[:, tc * 8:(tc + 1) * 8], prr[:, tc * 8:(tc + 1) * 8],
                                 sm3[:, 3:4], None, ALU.mult))
                t_prr = _IR.inc(rec, VC)
                sl = tc % 2
                rec = ir.add("tensor",
                             lambda e, tc=tc, sl=sl: e.transpose(
                                 pt2[sl][0:16, 0:P], ar_sb[:, tc * 24:tc * 24 + 16],
                                 sb_id[:, :]),
                             waits=[(SC.h, t_arev)] + pt2_free[sl])
                t_axT = _IR.inc(rec, MM)
                rec = ir.add("scalar",
                             lambda e, tc=tc, sl=sl: e.copy(axf[0:16, tc * P:(tc + 1) * P],
                                                            pt2[sl][0:16, 0:P]),
                             waits=[(MM.h, t_axT)])
                t_axe = _IR.inc(rec, SC)
                pt2_free[sl] = [(SC.h, t_axe)]
                pw = [(SC.h, t_axe)]
                if tc > 0:
                    pw.append((VC.h, t_mix[tc - 1]))
                rec = None
                for ex in range(8):
                    rec = ir.add("tensor",
                                 lambda e, tc=tc, ex=ex: e.matmul(
                                     pspe[:, ex * 16:(ex + 1) * 16],
                                     axf[0:16, tc * P:(tc + 1) * P],
                                     sb_cor[0:16, ex * 16:(ex + 1) * 16],
                                     start=(ex == 0), stop=(ex == 7)),
                                 waits=(pw if ex == 0 else []))
                t_pe = _IR.inc(rec, MM)
                ir.add("vector",
                       lambda e, tc=tc: e.tensor_scalar(
                           mix[:, tc * 16:(tc + 1) * 16], pspe[:, 0:16],
                           prr[:, tc * 8:tc * 8 + 1], 2.0, ALU.mult, ALU.mult),
                       waits=[(MM.h, t_pe), (VC.h, t_prr)])
                for ex in range(1, 8):
                    ir.add("vector",
                           lambda e, tc=tc, ex=ex: e.tensor_scalar(
                               t16[:, 0:16], pspe[:, ex * 16:(ex + 1) * 16],
                               prr[:, tc * 8 + ex:tc * 8 + ex + 1], 2.0,
                               ALU.mult, ALU.mult))
                    ir.add("vector",
                           lambda e, tc=tc: e.tensor_add(mix[:, tc * 16:(tc + 1) * 16],
                                                         mix[:, tc * 16:(tc + 1) * 16],
                                                         t16[:, 0:16]))
                rec = ir.ops["vector"][-1]
                t_mix[tc] = _IR.inc(rec, VC)
                rec = ir.add("tensor",
                             lambda e, tc=tc, sl=sl: e.transpose(
                                 pt2[sl][0:16, 0:P], mix[:, tc * 16:(tc + 1) * 16],
                                 sb_id[:, :]),
                             waits=[(VC.h, t_mix[tc])] + pt2_free[sl])
                t_mxT = _IR.inc(rec, MM)
                rec = ir.add("scalar",
                             lambda e, tc=tc, sl=sl: e.copy(mxf[0:16, tc * P:(tc + 1) * P],
                                                            pt2[sl][0:16, 0:P]),
                             waits=[(MM.h, t_mxT)])
                t_mxf = _IR.inc(rec, SC)
                pt2_free[sl] = [(SC.h, t_mxf)]
            t_mix_last = t_mix[3]
            if hf == 0:
                _stop_early(7, [(SC.h, t_mxf), (VC.h, t_mix_last)])

        with contextlib.ExitStack() as esb:
            psb = [esb.enter_context(nc.psum_tensor(f"psb{hf}_{i}", [P, 512], F32))
                   for i in range(8)]
            yb_free = [[], []]
            bfree = [[(SC.h, t_mxf), (VC.h, t_mix_last)] for _ in range(8)]
            t_fev = None
            for og2 in range(4):
                for j in range(2):
                    for tc in range(4):
                        ir.add("tensor",
                               lambda e, j=j, tc=tc, og2=og2: e.matmul(
                                   psb[j * 4 + tc][:, :], sb_on[0:1, :],
                                   sb_bb[0:1, (2 * og2 + j) * 512:(2 * og2 + j + 1) * 512],
                                   start=True, stop=False),
                               waits=bfree[j * 4 + tc])
                t_mm = None
                for cc in range(NCC):
                    sl = cc % 2
                    t_y = dma("sync", LY[sl], ysb[sl][:, :],
                              d_sx2[cc * P:(cc + 1) * P, hf * 512:hf * 512 + 512],
                              waits=[(SX[sl].h, t_sx[cc])] + yb_free[sl])
                    t_w = None
                    for j in range(2):
                        t_w = dma("sync", LW3[sl], wpb[sl][:, j, :],
                                  d_wb[cc, 2 * og2 + j, :, :])
                    first = [(LY[sl].h, t_y), (LW3[sl].h, t_w)]
                    rec = None
                    for j in range(2):
                        for tc in range(4):
                            rec = ir.add("tensor",
                                         lambda e, sl=sl, j=j, tc=tc, cc=cc: e.matmul(
                                             psb[j * 4 + tc][:, :],
                                             ysb[sl][:, tc * P:(tc + 1) * P],
                                             wpb[sl][:, j, :],
                                             start=False, stop=False),
                                         waits=(first if (j == 0 and tc == 0) else []))
                    t_mm = _IR.inc(rec, MM)
                    yb_free[sl] = [(MM.h, t_mm)]
                rec = None
                t_lora = [None] * 8
                for j in range(2):
                    for tc in range(4):
                        rec = ir.add("tensor",
                                     lambda e, j=j, tc=tc, og2=og2: e.matmul(
                                         psb[j * 4 + tc][:, :],
                                         mxf[0:16, tc * P:(tc + 1) * P],
                                         sb_wBT[0:16,
                                                (2 * og2 + j) * 512:(2 * og2 + j + 1) * 512],
                                         start=False, stop=True),
                                     waits=([(SC.h, t_mxf)]
                                            if (j == 0 and tc == 0) else []))
                        t_lora[j * 4 + tc] = _IR.inc(rec, MM)
                for j in range(2):
                    for tc in range(4):
                        og = 2 * og2 + j
                        rec = ir.add("vector",
                                     lambda e, j=j, tc=tc, og=og: e.tensor_add(
                                         res[tc][:, og * 512:(og + 1) * 512],
                                         res[tc][:, og * 512:(og + 1) * 512],
                                         psb[j * 4 + tc][:, :]),
                                     waits=[(MM.h, t_lora[j * 4 + tc])])
                        t_fev = _IR.inc(rec, VC)
                        bfree[j * 4 + tc] = [(VC.h, t_fev)]

        if hf == 0:
            _stop_early(8, [(VC.h, t_fev)])
        t_ln2 = [None] * 4
        for tc in range(4):
            emit_ln(tc, (VC.h, t_fev), t_ln2)
        for tc in range(4):
            dma("gpsimd", OUT, d_out[hf * 512 + tc * P:hf * 512 + (tc + 1) * P, :],
                res[tc][:, :], waits=[(VC.h, t_ln2[tc])])
    es3.close()

    ir.add("gpsimd", lambda e: e.engine_nop(), waits=[(OUT.h, OUT.v)])

    with nc.Block() as blk:
        for name in ENGS:
            _self = ir.auto.get(name)
            def runner(eng, _ops=tuple(ir.ops[name]), _self=_self):
                n = 0
                for fn, waits, incs in _ops:
                    for (hh, vv) in waits:
                        eng.wait_ge(hh, vv)
                    if _self is not None and n > 0:
                        eng.wait_ge(_self.h, n)
                    inst = fn(eng)
                    for (hh, n2) in incs:
                        inst.then_inc(hh, n2)
                    n += 1
            getattr(blk, name)(runner)
    return nc


def _tables():
    th = 1.0 / (10000.0 ** (np.arange(0, HD, 2, dtype=np.float64) / HD))
    idx = np.outer(np.arange(T, dtype=np.float64), th)
    cos2 = np.concatenate([np.cos(idx), np.cos(idx)], axis=1).T
    sin2 = np.concatenate([np.sin(idx), np.sin(idx)], axis=1).T
    sin2[:64, :] = -sin2[:64, :]
    cosf = np.ones((P, TA), dtype=np.float64)
    sinf = np.zeros((P, TA), dtype=np.float64)
    cosf[:, :T] = cos2
    sinf[:, :T] = sin2
    out = []
    for t64 in (cosf, sinf):
        hi = t64.astype(np.float32)
        lo = (t64 - hi.astype(np.float64)).astype(np.float32)
        out += [np.ascontiguousarray(hi), np.ascontiguousarray(lo)]
    return out


_NC_CACHE = [None]


def kernel(L0_attn_x, L0_attn_w_attn, L0_attn_w_proj, L0_attn_adapter_wte, L0_attn_gating,
           L0_ffn_W_base, L0_ffn_b_base, L0_ffn_A_w, L0_ffn_B_w, L0_ffn_router_w,
           L0_ffn_cores):
    f = np.float32
    x = np.ascontiguousarray(np.asarray(L0_attn_x, f))
    wat = np.asarray(L0_attn_w_attn, f)
    wpr = np.asarray(L0_attn_w_proj, f)
    adw = np.asarray(L0_attn_adapter_wte, f)
    gat = np.ascontiguousarray(np.asarray(L0_attn_gating, f).reshape(1, 1))
    wba = np.asarray(L0_ffn_W_base, f)
    bb = np.ascontiguousarray(np.asarray(L0_ffn_b_base, f).reshape(1, C))
    aw = np.asarray(L0_ffn_A_w, f)
    bw = np.asarray(L0_ffn_B_w, f)
    rw = np.asarray(L0_ffn_router_w, f)
    cor = np.asarray(L0_ffn_cores, f)

    adT = adw.T
    xfm = {}
    for b in range(B):
        xc = np.concatenate([x[b].T, adT], axis=1)
        xfm[b] = np.ascontiguousarray(
            xc.reshape(NCC, P, TA).transpose(1, 0, 2))
    wqkv = {}
    for s in range(2):
        rows = np.concatenate([wat[2048 * s:2048 * s + 2048],
                               wat[C + 2048 * s:C + 2048 * s + 2048],
                               wat[2 * C + 2048 * s:2 * C + 2048 * s + 2048]], axis=0)
        wqkv[s] = np.ascontiguousarray(
            rows.T.reshape(NCC, P, NOC, P).transpose(2, 1, 0, 3))
    wp4 = {}
    for s in range(2):
        wp4[s] = np.ascontiguousarray(
            wpr.T[2048 * s:2048 * s + 2048].reshape(16, P, 8, 512).transpose(0, 2, 1, 3))
    wb4 = np.ascontiguousarray(
        wba.T.reshape(NCC, P, 8, 512).transpose(0, 2, 1, 3))
    war = np.ascontiguousarray(
        np.concatenate([aw.T, rw.T], axis=1).reshape(NCC, P, 24).transpose(1, 0, 2))
    wBT = np.ascontiguousarray(bw.T)
    corr = np.ascontiguousarray(cor.transpose(1, 0, 2).reshape(16, P))
    ch, cl, sh, slo = _tables()
    ident = np.eye(P, dtype=f)
    ones1 = np.ones((1, P), dtype=f)
    mask = np.where(np.arange(P)[None, :] <= np.arange(P)[:, None], f(0.0), f(NEG)).astype(f)

    if _NC_CACHE[0] is None:
        _NC_CACHE[0] = _build()
    nc = _NC_CACHE[0]

    in_maps = []
    for rank in range(8):
        b, s = rank // 2, rank % 2
        in_maps.append({
            "xfm3": xfm[b],
            "xtm": np.ascontiguousarray(x[b, s * 1024:(s + 1) * 1024, :]),
            "wqkv4": wqkv[s],
            "wp4": wp4[s],
            "wb4": wb4,
            "war3": war,
            "wBT": wBT,
            "bb": bb,
            "coresr": corr,
            "gat": gat,
            "cos_hi": ch, "cos_lo": cl, "sin_hi": sh, "sin_lo": slo,
            "ident": ident, "ones1": ones1, "mask": mask,
        })
    resu = run_bass_kernel_spmd(nc, in_maps, core_ids=list(range(8)))
    out = np.empty((B, T, C), dtype=f)
    for rank in range(8):
        b, s = rank // 2, rank % 2
        out[b, s * 1024:(s + 1) * 1024, :] = resu.results[rank]["out"]
    return out


# version 08
# speedup vs baseline: 1.2611x; 1.0702x over previous
import math
import contextlib
import numpy as np

import concourse.bass as bass
import concourse.mybir as mybir
from concourse.bass_utils import run_bass_kernel_spmd

F32 = mybir.dt.float32
AF = mybir.ActivationFunctionType
ALU = mybir.AluOpType
AX = mybir.AxisListType

P = 128
B, T, C = 4, 2048, 4096
H, HD, HPC = 32, 128, 16
AT = 10
TA = T + AT
NCC = C // P
NQB = T // P
NOC = 3 * HPC
SC_QK = 1.0 / math.sqrt(HD)
SC_AD = 1.0 / math.sqrt(C)
EPS = 1e-5
NEG = -1.0e30
TILE_W = (512, 512, 512, 522)
RG = [[0, 1], [2, 3], [4, 5], [6, 7]]

ENGS = ("sync", "gpsimd", "scalar", "vector", "tensor")


class _Sem:
    __slots__ = ("h", "v")

    def __init__(self, nc, name):
        self.h = nc.alloc_semaphore(name=name)
        self.v = 0


class _IR:

    def __init__(self):
        self.ops = {e: [] for e in ENGS}
        self.auto = {}
        self.stopped = False

    def add(self, eng, fn, waits=()):
        if self.stopped:
            return [None, [], []]
        rec = [fn, [w for w in waits if w is not None and w[1] is not None], []]
        sem = self.auto.get(eng)
        if sem is not None:
            rec[2].append((sem.h, 1))
            sem.v += 1
        self.ops[eng].append(rec)
        return rec

    @staticmethod
    def inc(rec, sem, n=1):
        if rec[0] is None:
            return sem.v
        for (hh, _n) in rec[2]:
            if hh is sem.h:
                return sem.v
        rec[2].append((sem.h, n))
        sem.v += n
        return sem.v


class _Stop(Exception):
    pass


def _build(trunc=9):
    nc = bass.Bass("TRN2", target_bir_lowering=False)
    ir = _IR()
    MM = _Sem(nc, "MM")
    SC = _Sem(nc, "SC")
    VC = _Sem(nc, "VC")
    ir.auto = {"vector": VC, "scalar": SC}
    CCS = _Sem(nc, "CCS")
    LDC = _Sem(nc, "LDC")
    LDX = _Sem(nc, "LDX")
    LDW = [_Sem(nc, f"LDW{i}") for i in range(2)]
    LQ = [_Sem(nc, f"LQ{i}") for i in range(2)]
    LK = [_Sem(nc, f"LK{i}") for i in range(2)]
    LV = [_Sem(nc, f"LV{i}") for i in range(2)]
    LY = [_Sem(nc, f"LY{i}") for i in range(4)]
    LW3 = [_Sem(nc, f"LW3{i}") for i in range(4)]
    LDP = _Sem(nc, "LDP")
    LDX2 = _Sem(nc, "LDX2")
    SW = [_Sem(nc, f"SW{i}") for i in range(2)]
    SP = [_Sem(nc, f"SP{i}") for i in range(2)]
    YS = [_Sem(nc, f"YS{i}") for i in range(2)]
    SG = [_Sem(nc, f"SG{i}") for i in range(4)]
    LXT = [_Sem(nc, f"LXT{i}") for i in range(2)]
    SX = [_Sem(nc, f"SX{i}") for i in range(2)]
    OUT = _Sem(nc, "OUT")

    d_xfm = nc.dram_tensor("xfm3", [P, NCC, TA], F32, kind="ExternalInput")
    d_xtm = nc.dram_tensor("xtm", [1024, C], F32, kind="ExternalInput")
    d_wqkv = nc.dram_tensor("wqkv4", [NOC, P, NCC, P], F32, kind="ExternalInput")
    d_wp = nc.dram_tensor("wp4", [16, 8, P, 512], F32, kind="ExternalInput")
    d_wb = nc.dram_tensor("wb4", [NCC, 8, P, 512], F32, kind="ExternalInput")
    d_war = nc.dram_tensor("war3", [P, NCC, 24], F32, kind="ExternalInput")
    d_wBT = nc.dram_tensor("wBT", [16, C], F32, kind="ExternalInput")
    d_bb = nc.dram_tensor("bb", [1, C], F32, kind="ExternalInput")
    d_cor = nc.dram_tensor("coresr", [16, P], F32, kind="ExternalInput")
    d_gat = nc.dram_tensor("gat", [1, 1], F32, kind="ExternalInput")
    d_ch = nc.dram_tensor("cos_hi", [P, TA], F32, kind="ExternalInput")
    d_cl = nc.dram_tensor("cos_lo", [P, TA], F32, kind="ExternalInput")
    d_sh = nc.dram_tensor("sin_hi", [P, TA], F32, kind="ExternalInput")
    d_sl = nc.dram_tensor("sin_lo", [P, TA], F32, kind="ExternalInput")
    d_id = nc.dram_tensor("ident", [P, P], F32, kind="ExternalInput")
    d_on = nc.dram_tensor("ones1", [1, P], F32, kind="ExternalInput")
    d_mk = nc.dram_tensor("mask", [P, P], F32, kind="ExternalInput")
    d_out = nc.dram_tensor("out", [1024, C], F32, kind="ExternalOutput")
    d_sqkv = nc.dram_tensor("sqkv", [NOC * P, TA], F32)
    d_sy = nc.dram_tensor("sy", [HPC * P, T], F32)
    d_psnd = [nc.dram_tensor(f"psend{i}", [2, 512, C], F32) for i in range(2)]
    d_prcv = [nc.dram_tensor(f"precv{i}", [512, C], F32) for i in range(2)]
    d_sx2 = nc.dram_tensor("sx2", [C, 1024], F32)

    sb_id = nc.alloc_sbuf_tensor("sb_id", [P, P], F32)
    sb_on = nc.alloc_sbuf_tensor("sb_on", [1, P], F32)
    sb_mk = nc.alloc_sbuf_tensor("sb_mk", [P, P], F32)
    sb_gat = nc.alloc_sbuf_tensor("sb_gat", [1, 1], F32)
    sb_gb = nc.alloc_sbuf_tensor("sb_gb", [P, 1], F32)
    sb_bb = nc.alloc_sbuf_tensor("sb_bb", [1, C], F32)
    sb_war = nc.alloc_sbuf_tensor("sb_war", [P, NCC, 24], F32)
    sb_wBT = nc.alloc_sbuf_tensor("sb_wBT", [16, C], F32)
    sb_cor = nc.alloc_sbuf_tensor("sb_cor", [16, P], F32)

    def dma(eng, sem, dst, src, waits=(), accum=None):
        if accum is None:
            rec = ir.add(eng, lambda e, d=dst, s=src: e.dma_start(out=d, in_=s), waits)
        else:
            rec = ir.add(eng,
                         lambda e, d=dst, s=src, a=accum: e.dma_start(out=d, in_=s,
                                                                      accum_op=a),
                         waits)
        return _IR.inc(rec, sem, 16)

    dma("sync", LDC, sb_id[:, :], d_id[:, :])
    dma("sync", LDC, sb_on[:, :], d_on[:, :])
    dma("sync", LDC, sb_mk[:, :], d_mk[:, :])
    dma("sync", LDC, sb_gat[:, :], d_gat[:, :])
    dma("sync", LDC, sb_bb[:, :], d_bb[:, :])
    dma("sync", LDC, sb_war[:, :, :], d_war[:, :, :])
    dma("sync", LDC, sb_wBT[:, :], d_wBT[:, :])
    dma("sync", LDC, sb_cor[:, :], d_cor[:, :])

    es1 = contextlib.ExitStack()
    xf = es1.enter_context(nc.sbuf_tensor("xf", [P, NCC, 522], F32))
    wsb = [es1.enter_context(nc.sbuf_tensor(f"w1_{i}", [P, NCC, P], F32)) for i in range(2)]
    ev = [es1.enter_context(nc.sbuf_tensor(f"ev{i}", [P, 522], F32)) for i in range(2)]
    sw = [es1.enter_context(nc.sbuf_tensor(f"sw{i}", [P, 522], F32)) for i in range(2)]
    rp = [es1.enter_context(nc.sbuf_tensor(f"rp{i}", [P, 522], F32)) for i in range(2)]
    tmp1 = es1.enter_context(nc.sbuf_tensor("tmp1", [P, 522], F32))
    cosf = es1.enter_context(nc.sbuf_tensor("cosf", [P, TA], F32))
    sinf = es1.enter_context(nc.sbuf_tensor("sinf", [P, TA], F32))
    clo = es1.enter_context(nc.sbuf_tensor("clo", [P, TA], F32))
    ps1 = [es1.enter_context(nc.psum_tensor(f"ps1_{i}", [P, 1024], F32)) for i in range(2)]

    dma("sync", LDC, cosf[:, :], d_ch[:, :])
    dma("sync", LDC, clo[:, :], d_cl[:, :])
    t_tab1 = dma("sync", LDC, sinf[:, :], d_sh[:, :])
    rec = ir.add("vector", lambda e: e.tensor_add(cosf[:, :], cosf[:, :], clo[:, :]),
                 waits=[(LDC.h, t_tab1)])
    t_cos = _IR.inc(rec, VC)
    t_tab2 = dma("sync", LDC, clo[:, :], d_sl[:, :], waits=[(VC.h, t_cos)])
    rec = ir.add("vector", lambda e: e.tensor_add(sinf[:, :], sinf[:, :], clo[:, :]),
                 waits=[(LDC.h, t_tab2)])
    _IR.inc(rec, VC)
    t_const = t_tab1

    with nc.psum_tensor("ps0", [P, 512], F32) as ps0:
        rec = ir.add("tensor",
                     lambda e: e.matmul(ps0[:, 0:1], sb_on[0:1, :], sb_gat[0:1, 0:1],
                                        start=True, stop=True),
                     waits=[(LDC.h, t_const)])
        t_mm0 = _IR.inc(rec, MM)
        rec = ir.add("scalar", lambda e: e.copy(sb_gb[:, 0:1], ps0[:, 0:1]),
                     waits=[(MM.h, t_mm0)])
        t_sc0 = _IR.inc(rec, SC)
    t_ph0_sc = t_sc0

    def _stop_early(level, waits):
        if trunc <= level and not ir.stopped:
            rec = ir.add("gpsimd",
                         lambda e: e.dma_start(out=d_out[:, :], in_=d_xtm[:, :]),
                         waits)
            _IR.inc(rec, OUT, 16)
            ir.stopped = True

    w_free = [None, None]
    ps_free = [t_ph0_sc, t_ph0_sc]
    ev_free = [[], []]
    sw_free = [[], []]
    rp_free = [[], []]
    last_mm_tile = None
    for tt in range(4):
        W = TILE_W[tt]
        t0 = 512 * tt
        t_w0 = dma("sync", LDW[0], wsb[0][:, :, :], d_wqkv[0, :, :, :],
                   waits=([(MM.h, w_free[0])] if w_free[0] is not None else []))
        xw = [(MM.h, last_mm_tile)] if last_mm_tile is not None else []
        t_x = dma("sync", LDX, xf[:, 0:16, 0:W], d_xfm[:, 0:16, t0:t0 + W], waits=xw)
        t_x2 = dma("sync", LDX2, xf[:, 16:NCC, 0:W], d_xfm[:, 16:NCC, t0:t0 + W])
        for oc in range(NOC):
            sl = oc % 2
            if oc == 0:
                t_w = t_w0
            else:
                t_w = dma("sync", LDW[sl], wsb[sl][:, :, :], d_wqkv[oc, :, :, :],
                          waits=([(MM.h, w_free[sl])] if w_free[sl] is not None else []))
            first = [(LDW[sl].h, t_w), (LDX.h, t_x), (SC.h, ps_free[sl])]
            rec = None
            for cc in range(NCC):
                cw = first if cc == 0 else (
                    [(LDX2.h, t_x2)] if (cc == 16 and oc == 0) else [])
                rec = ir.add("tensor",
                             lambda e, sl=sl, cc=cc: e.matmul(
                                 ps1[sl][:, 0:512], wsb[sl][:, cc, :], xf[:, cc, 0:512],
                                 start=(cc == 0), stop=(cc == NCC - 1)),
                             waits=cw)
            if W > 512:
                for cc in range(NCC):
                    rec = ir.add("tensor",
                                 lambda e, sl=sl, cc=cc, W=W: e.matmul(
                                     ps1[sl][:, 512:W], wsb[sl][:, cc, :], xf[:, cc, 512:W],
                                     start=(cc == 0), stop=(cc == NCC - 1)))
            t_mm = _IR.inc(rec, MM)
            w_free[sl] = t_mm
            rec = ir.add("scalar",
                         lambda e, sl=sl, W=W: e.copy(ev[sl][:, 0:W], ps1[sl][:, 0:W]),
                         waits=[(MM.h, t_mm)] + ev_free[sl])
            t_ev = _IR.inc(rec, SC)
            ps_free[sl] = t_ev
            if oc < 2 * HPC:
                rec = ir.add("gpsimd",
                             lambda e, sl=sl, W=W: e.dma_start(out=sw[sl][0:64, 0:W],
                                                               in_=ev[sl][64:128, 0:W]),
                             waits=[(SC.h, t_ev)] + sw_free[sl])
                _IR.inc(rec, SW[sl], 16)
                rec = ir.add("gpsimd",
                             lambda e, sl=sl, W=W: e.dma_start(out=sw[sl][64:128, 0:W],
                                                               in_=ev[sl][0:64, 0:W]))
                t_swp = _IR.inc(rec, SW[sl], 16)
                ir.add("vector",
                       lambda e, sl=sl, W=W, t0=t0: e.tensor_mul(
                           tmp1[:, 0:W], sw[sl][:, 0:W], sinf[:, t0:t0 + W]),
                       waits=[(SW[sl].h, t_swp)] + rp_free[sl])
                ir.add("vector",
                       lambda e, sl=sl, W=W, t0=t0: e.tensor_mul(
                           rp[sl][:, 0:W], ev[sl][:, 0:W], cosf[:, t0:t0 + W]))
                rec = ir.add("vector",
                             lambda e, sl=sl, W=W: e.tensor_add(
                                 rp[sl][:, 0:W], rp[sl][:, 0:W], tmp1[:, 0:W]))
                t_rope = _IR.inc(rec, VC)
                sp_prev = (SP[sl].h, SP[sl].v) if SP[sl].v else None
                t_sp = dma("gpsimd", SP[sl], d_sqkv[oc * P:(oc + 1) * P, t0:t0 + W],
                           rp[sl][:, 0:W], waits=[(VC.h, t_rope), sp_prev])
                ev_free[sl] = [(SW[sl].h, t_swp), (VC.h, t_rope)]
                sw_free[sl] = [(VC.h, t_rope)]
                rp_free[sl] = [(SP[sl].h, t_sp)]
            else:
                sp_prev = (SP[sl].h, SP[sl].v) if SP[sl].v else None
                t_sp = dma("gpsimd", SP[sl], d_sqkv[oc * P:(oc + 1) * P, t0:t0 + W],
                           ev[sl][:, 0:W], waits=[(SC.h, t_ev), sp_prev])
                ev_free[sl] = [(SP[sl].h, t_sp)]
        last_mm_tile = t_mm
    t_p1_done = [(SP[0].h, SP[0].v), (SP[1].h, SP[1].v)]
    es1.close()
    _stop_early(1, list(t_p1_done))

    es2 = contextlib.ExitStack()
    qf = [es2.enter_context(nc.sbuf_tensor(f"qf{i}", [P, T], F32)) for i in range(2)]
    kf = [es2.enter_context(nc.sbuf_tensor(f"kf{i}", [P, TA], F32)) for i in range(2)]
    vf = [es2.enter_context(nc.sbuf_tensor(f"vf{i}", [P, TA], F32)) for i in range(2)]
    vt = es2.enter_context(nc.sbuf_tensor("vt", [P, 17 * P], F32))
    prb = es2.enter_context(nc.sbuf_tensor("prb", [P, T], F32))
    pra = es2.enter_context(nc.sbuf_tensor("pra", [P, 16], F32))
    ptb = [es2.enter_context(nc.sbuf_tensor(f"ptb{i}", [P, P], F32)) for i in range(2)]
    pta = es2.enter_context(nc.sbuf_tensor("pta", [16, P], F32))
    yst = [es2.enter_context(nc.sbuf_tensor(f"yst{i}", [P, T], F32)) for i in range(2)]
    sml = es2.enter_context(nc.sbuf_tensor("sml", [P, 12], F32))
    pss = es2.enter_context(nc.psum_tensor("pss", [P, 2048], F32))
    psa = es2.enter_context(nc.psum_tensor("psa", [P, 512], F32))
    psy = es2.enter_context(nc.psum_tensor("psy", [P, 512], F32))
    ptp = [es2.enter_context(nc.psum_tensor(f"ptp{i}", [P, 512], F32)) for i in range(2)]

    qkv_free = [None, None]
    ptp_free = [[], []]
    yst_free = [None, None]
    for h in range(HPC):
        sl = h % 2
        lw = list(t_p1_done) + ([(MM.h, qkv_free[sl])] if qkv_free[sl] is not None else [])
        t_q = dma("sync", LQ[sl], qf[sl][:, :], d_sqkv[h * P:(h + 1) * P, 0:T], waits=lw)
        t_k = dma("sync", LK[sl], kf[sl][:, :], d_sqkv[(HPC + h) * P:(HPC + h + 1) * P, :])
        t_v = dma("sync", LV[sl], vf[sl][:, :],
                  d_sqkv[(2 * HPC + h) * P:(2 * HPC + h + 1) * P, :])
        def emit_vtr(j):
            bw = P if j < 16 else AT
            tsl = j % 2
            rec = ir.add("tensor",
                         lambda e, sl=sl, j=j, bw=bw, tsl=tsl: e.transpose(
                             ptp[tsl][0:bw, 0:P], vf[sl][:, j * P:j * P + bw], sb_id[:, :]),
                         waits=(([(LV[sl].h, t_v), (LDC.h, t_const)] if j == 0 else []) +
                                ptp_free[tsl]))
            return _IR.inc(rec, MM)

        def emit_vev(j, t_tr):
            bw = P if j < 16 else AT
            tsl = j % 2
            if j % 2 == 0:
                rec = ir.add("scalar",
                             lambda e, j=j, bw=bw, tsl=tsl: e.copy(
                                 vt[0:bw, j * P:(j + 1) * P], ptp[tsl][0:bw, 0:P]),
                             waits=[(MM.h, t_tr)])
                ptp_free[tsl] = [(SC.h, _IR.inc(rec, SC))]
            else:
                rec = ir.add("vector",
                             lambda e, j=j, bw=bw, tsl=tsl: e.tensor_copy(
                                 vt[0:bw, j * P:(j + 1) * P], ptp[tsl][0:bw, 0:P]),
                             waits=[(MM.h, t_tr)])
                ptp_free[tsl] = [(VC.h, _IR.inc(rec, VC))]

        t_trv = [None] * 17
        for j in range(17):
            t_trv[j] = emit_vtr(j)
            if j >= 1:
                emit_vev(j - 1, t_trv[j - 1])
        emit_vev(16, t_trv[16])

        t_exp = [None] * NQB
        t_expa = [None] * NQB
        t_max = [None] * NQB
        t_amax = [None] * NQB
        t_pn = [None] * NQB
        t_lastT = [None] * NQB
        t_ymm = [None] * NQB
        t_yev = [None] * NQB
        ptb_free = [[], []]

        t_smm_arr = [None] * NQB
        t_kg_arr = [None] * NQB

        def emit_scores_mm(i, pv):
            KL = (i + 1) * P
            first = [(LQ[sl].h, t_q), (LK[sl].h, t_k)] if pv is None else []
            if pv is not None:
                first += [(SC.h, t_exp[pv]), (VC.h, t_max[pv])]
            ng = (KL + 511) // 512
            t_kg_arr[i] = [None] * ng
            for kg in range(ng):
                kw = min(512, KL - kg * 512)
                rec = ir.add("tensor",
                             lambda e, sl=sl, i=i, kg=kg, kw=kw: e.matmul(
                                 pss[:, kg * 512:kg * 512 + kw],
                                 qf[sl][:, i * P:(i + 1) * P],
                                 kf[sl][:, kg * 512:kg * 512 + kw],
                                 start=True, stop=True),
                             waits=(first if kg == 0 else []))
                t_kg_arr[i][kg] = _IR.inc(rec, MM)
            aw = [] if pv is None else [(SC.h, t_expa[pv]), (VC.h, t_amax[pv])]
            rec = ir.add("tensor",
                         lambda e, sl=sl, i=i: e.matmul(
                             psa[:, 0:AT], qf[sl][:, i * P:(i + 1) * P],
                             kf[sl][:, T:TA], start=True, stop=True),
                         waits=aw)
            t_smm_arr[i] = _IR.inc(rec, MM)

        def emit_softmax(i, pv):
            KL = (i + 1) * P
            t_smm = t_smm_arr[i]
            ng = (KL + 511) // 512
            for kg in range(ng):
                kw = min(512, KL - kg * 512)
                if kg == ng - 1:
                    ir.add("vector",
                           lambda e, i=i: e.tensor_add(pss[:, i * P:(i + 1) * P],
                                                       pss[:, i * P:(i + 1) * P],
                                                       sb_mk[:, :]),
                           waits=[(MM.h, t_kg_arr[i][kg])])
                    ir.add("vector",
                           lambda e, kg=kg, kw=kw: e.reduce_max(
                               sml[:, 8 + kg:9 + kg], pss[:, kg * 512:kg * 512 + kw],
                               AX.X))
                else:
                    ir.add("vector",
                           lambda e, kg=kg, kw=kw: e.reduce_max(
                               sml[:, 8 + kg:9 + kg], pss[:, kg * 512:kg * 512 + kw],
                               AX.X),
                           waits=[(MM.h, t_kg_arr[i][kg])])
            ir.add("vector",
                   lambda e, ng=ng: e.reduce_max(sml[:, 0:1], sml[:, 8:8 + ng],
                                                 AX.X, negate=True))
            rec = ir.add("vector",
                         lambda e: e.tensor_scalar_mul(sml[:, 0:1], sml[:, 0:1], SC_QK))
            t_max[i] = _IR.inc(rec, VC)
            ew = [(VC.h, t_max[i])]
            if pv is not None:
                ew.append((MM.h, t_lastT[pv]))
            rec = ir.add("scalar",
                         lambda e, KL=KL: e.activation(prb[:, 0:KL], pss[:, 0:KL], AF.Exp,
                                                       bias=sml[:, 0:1], scale=SC_QK,
                                                       accum_out=sml[:, 1:2]),
                         waits=ew)
            t_exp[i] = _IR.inc(rec, SC)
            ir.add("vector",
                   lambda e: e.reduce_max(sml[:, 3:4], psa[:, 0:AT], AX.X, negate=True),
                   waits=[(MM.h, t_smm)])
            rec = ir.add("vector",
                         lambda e: e.tensor_scalar_mul(sml[:, 3:4], sml[:, 3:4], SC_AD))
            t_amax[i] = _IR.inc(rec, VC)
            rec = ir.add("scalar",
                         lambda e: e.activation(pra[:, 0:AT], psa[:, 0:AT], AF.Exp,
                                                bias=sml[:, 3:4], scale=SC_AD,
                                                accum_out=sml[:, 4:5]),
                         waits=[(VC.h, t_amax[i])])
            t_expa[i] = _IR.inc(rec, SC)
            ir.add("vector", lambda e: e.reciprocal(sml[:, 2:3], sml[:, 1:2]),
                   waits=[(SC.h, t_exp[i])])
            ir.add("vector",
                   lambda e, KL=KL: e.tensor_scalar(prb[:, 0:KL], prb[:, 0:KL],
                                                    sml[:, 2:3], None, ALU.mult))
            ir.add("vector", lambda e: e.reciprocal(sml[:, 5:6], sml[:, 4:5]),
                   waits=[(SC.h, t_expa[i])])
            ir.add("vector", lambda e: e.tensor_mul(sml[:, 6:7], sml[:, 5:6], sb_gb[:, 0:1]),
                   waits=[(SC.h, t_ph0_sc)])
            rec = ir.add("vector",
                         lambda e: e.tensor_scalar(pra[:, 0:AT], pra[:, 0:AT],
                                                   sml[:, 6:7], None, ALU.mult))
            t_pn[i] = _IR.inc(rec, VC)

        def emit_y(i, pv, first_qb):
            t_pe = [None] * (i + 1)

            def emit_T(kb):
                tsl = kb % 2
                rec = ir.add("tensor",
                             lambda e, kb=kb, tsl=tsl: e.transpose(
                                 ptp[tsl][:, 0:P], prb[:, kb * P:(kb + 1) * P],
                                 sb_id[:, :]),
                             waits=([(VC.h, t_pn[i])] if kb == 0 else []) + ptp_free[tsl])
                return _IR.inc(rec, MM)

            def emit_ev(kb, t_tr):
                tsl = kb % 2
                if kb % 2 == 0:
                    rec = ir.add("scalar",
                                 lambda e, tsl=tsl: e.copy(ptb[tsl][:, :],
                                                           ptp[tsl][:, 0:P]),
                                 waits=[(MM.h, t_tr)] + ptb_free[tsl])
                    t_pe[kb] = (SC.h, _IR.inc(rec, SC))
                else:
                    rec = ir.add("vector",
                                 lambda e, tsl=tsl: e.tensor_copy(ptb[tsl][:, :],
                                                                  ptp[tsl][:, 0:P]),
                                 waits=[(MM.h, t_tr)] + ptb_free[tsl])
                    t_pe[kb] = (VC.h, _IR.inc(rec, VC))
                ptp_free[tsl] = [t_pe[kb]]

            def emit_Y(kb):
                tsl = kb % 2
                yw = [t_pe[kb]]
                if kb == 0 and pv is not None and t_yev[pv] is not None:
                    yw.append((SC.h, t_yev[pv]))
                rec = ir.add("tensor",
                             lambda e, kb=kb, tsl=tsl: e.matmul(
                                 psy[:, 0:P], vt[:, kb * P:(kb + 1) * P], ptb[tsl][:, :],
                                 start=(kb == 0), stop=False),
                             waits=yw)
                ptb_free[tsl] = [(MM.h, _IR.inc(rec, MM))]

            t_trs = [None] * (i + 1)
            t_trs[0] = emit_T(0)
            for kb in range(1, i + 1):
                t_trs[kb] = emit_T(kb)
                emit_ev(kb - 1, t_trs[kb - 1])
                emit_Y(kb - 1)
            tsl = (i + 1) % 2
            rec = ir.add("tensor",
                         lambda e, tsl=tsl: e.transpose(ptp[tsl][0:AT, 0:P],
                                                        pra[:, 0:AT], sb_id[:, :]),
                         waits=ptp_free[tsl])
            t_tra = _IR.inc(rec, MM)
            t_lastT[i] = t_tra
            emit_ev(i, t_trs[i])
            emit_Y(i)
            rec = ir.add("scalar",
                         lambda e, tsl=tsl: e.copy(pta[0:AT, :], ptp[tsl][0:AT, 0:P]),
                         waits=[(MM.h, t_tra)])
            t_pae = _IR.inc(rec, SC)
            ptp_free[tsl] = [(SC.h, t_pae)]
            rec = ir.add("tensor",
                         lambda e: e.matmul(psy[:, 0:P], vt[0:AT, 16 * P:17 * P],
                                            pta[0:AT, :], start=False, stop=True),
                         waits=[(SC.h, t_pae)])
            t_ymm[i] = _IR.inc(rec, MM)
            yw = [(MM.h, t_ymm[i])]
            if h >= 2 and first_qb and yst_free[sl] is not None:
                yw.append((YS[sl].h, yst_free[sl]))
            rec = ir.add("scalar",
                         lambda e, sl=sl, i=i: e.copy(yst[sl][:, i * P:(i + 1) * P],
                                                      psy[:, 0:P]),
                         waits=yw)
            t_yev[i] = _IR.inc(rec, SC)

        order = list(range(NQB - 1, -1, -1))
        prev = None
        for idx in range(NQB + 1):
            if idx < NQB:
                emit_scores_mm(order[idx], prev)
            if idx >= 1:
                emit_y(order[idx - 1], order[idx - 2] if idx >= 2 else None,
                       first_qb=(idx == 1))
            if idx < NQB:
                emit_softmax(order[idx], prev)
                prev = order[idx]
        t_ysp = dma("gpsimd", YS[sl], d_sy[h * P:(h + 1) * P, :], yst[sl][:, :],
                    waits=[(SC.h, t_yev[order[-1]])])
        yst_free[sl] = t_ysp
        qkv_free[sl] = t_ymm[order[-1]]
    t_p2_done = [(YS[0].h, YS[0].v), (YS[1].h, YS[1].v)]
    t_p2_sc = SC.v
    t_p2_vc = VC.v
    es2.close()
    _stop_early(2, list(t_p2_done))

    es3 = contextlib.ExitStack()
    ysb = [es3.enter_context(nc.sbuf_tensor(f"y3_{i}", [P, 512], F32)) for i in range(4)]
    wpb = [es3.enter_context(nc.sbuf_tensor(f"wp3_{i}", [P, 2, 512], F32)) for i in range(4)]
    stg = [es3.enter_context(nc.sbuf_tensor(f"stg{i}", [P, 512], F32)) for i in range(4)]

    with contextlib.ExitStack() as esp:
        psp = [esp.enter_context(nc.psum_tensor(f"psp{i}", [P, 512], F32)) for i in range(8)]
        y_free = [[] for _ in range(4)]
        stg_free = [[] for _ in range(4)]
        bank_free = [[(SC.h, t_p2_sc), (VC.h, t_p2_vc)] for _ in range(8)]
        t_cc = None
        for th in range(2):
            for sp in range(2):
                for og2 in range(4):
                    t_mm = None
                    for cc in range(16):
                        sl = cc % 4
                        t_y = dma("sync", LY[sl], ysb[sl][:, :],
                                  d_sy[cc * P:(cc + 1) * P,
                                       sp * 1024 + th * 512:sp * 1024 + th * 512 + 512],
                                  waits=list(t_p2_done) + y_free[sl])
                        t_w = None
                        for j in range(2):
                            t_w = dma("sync", LW3[sl], wpb[sl][:, j, :],
                                      d_wp[cc, 2 * og2 + j, :, :])
                        first = [(LY[sl].h, t_y), (LW3[sl].h, t_w)]
                        rec = None
                        for j in range(2):
                            for tc in range(4):
                                bw_ = (first if (j == 0 and tc == 0) else [])
                                if cc == 0:
                                    bw_ = bw_ + bank_free[j * 4 + tc]
                                rec = ir.add("tensor",
                                             lambda e, sl=sl, j=j, tc=tc, cc=cc: e.matmul(
                                                 psp[j * 4 + tc][:, :],
                                                 ysb[sl][:, tc * P:(tc + 1) * P],
                                                 wpb[sl][:, j, :],
                                                 start=(cc == 0), stop=(cc == 15)),
                                             waits=bw_)
                        t_mm = _IR.inc(rec, MM)
                        y_free[sl] = [(MM.h, t_mm)]
                    k = 0
                    for j in range(2):
                        for tc in range(4):
                            sl2 = k % 4
                            if k % 2 == 0:
                                rec = ir.add("scalar",
                                             lambda e, sl2=sl2, j=j, tc=tc: e.copy(
                                                 stg[sl2][:, :], psp[j * 4 + tc][:, :]),
                                             waits=[(MM.h, t_mm)] + stg_free[sl2])
                                t_e = (SC.h, _IR.inc(rec, SC))
                            else:
                                rec = ir.add("vector",
                                             lambda e, sl2=sl2, j=j, tc=tc: e.tensor_copy(
                                                 stg[sl2][:, :], psp[j * 4 + tc][:, :]),
                                             waits=[(MM.h, t_mm)] + stg_free[sl2])
                                t_e = (VC.h, _IR.inc(rec, VC))
                            bank_free[j * 4 + tc] = [t_e]
                            t_s = dma("sync", SG[sl2],
                                      d_psnd[th][sp, tc * P:(tc + 1) * P,
                                                 (2 * og2 + j) * 512:(2 * og2 + j + 1) * 512],
                                      stg[sl2][:, :], waits=[t_e])
                            stg_free[sl2] = [(SG[sl2].h, t_s)]
                            k += 1
            sgw = [(SG[i].h, SG[i].v) for i in range(4)]
            if th == 0:
                _stop_early(3, sgw)
            rec = ir.add("gpsimd",
                         lambda e, th=th: e.collective_compute(
                             "ReduceScatter", ALU.add, RG,
                             ins=[d_psnd[th][:, :, :]], outs=[d_prcv[th][:, :]]),
                         waits=sgw)
            t_cc = _IR.inc(rec, CCS)
        _stop_early(4, [(CCS.h, t_cc)])

    res = [es3.enter_context(nc.sbuf_tensor(f"res{i}", [P, C], F32)) for i in range(4)]
    trash = es3.enter_context(nc.sbuf_tensor("trash", [P, C], F32))
    sm3 = es3.enter_context(nc.sbuf_tensor("sm3", [P, 16], F32))
    nr3 = es3.enter_context(nc.sbuf_tensor("nr3", [P, 4], F32))
    rsm3 = es3.enter_context(nc.sbuf_tensor("rsm3", [P, 4], F32))
    prr = es3.enter_context(nc.sbuf_tensor("prr", [P, 32], F32))
    ar_sb = es3.enter_context(nc.sbuf_tensor("ar_sb", [P, 96], F32))
    mix = es3.enter_context(nc.sbuf_tensor("mix", [P, 64], F32))
    t16 = es3.enter_context(nc.sbuf_tensor("t16", [P, 16], F32))
    axf = es3.enter_context(nc.sbuf_tensor("axf", [16, 512], F32))
    mxf = es3.enter_context(nc.sbuf_tensor("mxf", [16, 512], F32))
    x2st = [es3.enter_context(nc.sbuf_tensor(f"x2st{i}", [P, 512], F32)) for i in range(2)]
    nm3 = es3.enter_context(nc.sbuf_tensor("nm3", [P, 4], F32))
    rs3 = es3.enter_context(nc.sbuf_tensor("rs3", [P, 4], F32))
    xts = [es3.enter_context(nc.sbuf_tensor(f"xts{i}", [P, C], F32)) for i in range(2)]

    def emit_ln(tc, gate, t_ln_out):
        ir.add("vector", lambda e, tc=tc: e.reduce_sum(sm3[:, 0:1], res[tc][:, :], AX.X),
               waits=[gate])
        rec = ir.add("scalar",
                     lambda e, tc=tc: e.activation(trash[:, :], res[tc][:, :], AF.Square,
                                                   accum_out=sm3[:, 4 + tc:5 + tc]),
                     waits=[gate])
        t_sq = _IR.inc(rec, SC)
        ir.add("vector",
               lambda e, tc=tc: e.tensor_scalar(nm3[:, tc:tc + 1], sm3[:, 0:1],
                                                -1.0 / C, None, ALU.mult))
        ir.add("vector",
               lambda e, tc=tc: e.tensor_mul(sm3[:, 1:2], nm3[:, tc:tc + 1],
                                             nm3[:, tc:tc + 1]))
        ir.add("vector",
               lambda e, tc=tc: e.tensor_scalar(sm3[:, 2:3], sm3[:, 4 + tc:5 + tc],
                                                1.0 / C, EPS, ALU.mult, ALU.add),
               waits=[(SC.h, t_sq)])
        rec = ir.add("vector",
                     lambda e: e.tensor_sub(sm3[:, 2:3], sm3[:, 2:3], sm3[:, 1:2]))
        t_ve = _IR.inc(rec, VC)
        rec = ir.add("scalar",
                     lambda e, tc=tc: e.activation(sm3[:, 8 + tc:9 + tc], sm3[:, 2:3],
                                                   AF.Sqrt),
                     waits=[(VC.h, t_ve)])
        t_sd = _IR.inc(rec, SC)
        ir.add("vector", lambda e, tc=tc: e.reciprocal(rs3[:, tc:tc + 1],
                                                       sm3[:, 8 + tc:9 + tc]),
               waits=[(SC.h, t_sd)])
        rec = ir.add("vector",
                     lambda e, tc=tc: e.tensor_scalar(res[tc][:, :], res[tc][:, :],
                                                      nm3[:, tc:tc + 1], rs3[:, tc:tc + 1],
                                                      ALU.add, ALU.mult))
        t_ln_out[tc] = _IR.inc(rec, VC)

    xts_free = [None, None]
    for hf in range(2):
        for tc in range(4):
            rw = [(CCS.h, hf + 1)]
            if hf == 1:
                rw.append((OUT.h, 64))
            dma("sync", LDP, res[tc][:, :],
                d_prcv[hf][tc * P:(tc + 1) * P, :], waits=rw)
        t_pl_all = LDP.v
        t_add = [None] * 4
        for tc in range(4):
            sl = tc % 2
            t_xt = dma("sync", LXT[sl], xts[sl][:, :],
                       d_xtm[hf * 512 + tc * P:hf * 512 + (tc + 1) * P, :],
                       waits=([xts_free[sl]] if xts_free[sl] is not None else []))
            rec = ir.add("vector",
                         lambda e, tc=tc, sl=sl: e.tensor_add(res[tc][:, :],
                                                              res[tc][:, :],
                                                              xts[sl][:, :]),
                         waits=[(LDP.h, t_pl_all), (LXT[sl].h, t_xt)])
            t_add[tc] = _IR.inc(rec, VC)
            xts_free[sl] = (VC.h, t_add[tc])
        if hf == 0:
            _stop_early(5, [(VC.h, t_add[3])])
        t_ln1 = [None] * 4
        for tc in range(4):
            emit_ln(tc, (VC.h, t_add[tc]), t_ln1)
        if hf == 0:
            _stop_early(6, [(VC.h, t_ln1[3])])

        with contextlib.ExitStack() as esb:
            pt2 = [esb.enter_context(nc.psum_tensor(f"pt2_{hf}_{i}", [P, 512], F32))
                   for i in range(2)]
            psar = esb.enter_context(nc.psum_tensor(f"psar{hf}", [P, 512], F32))
            pspe = esb.enter_context(nc.psum_tensor(f"pspe{hf}", [P, 512], F32))
            pt2_free = [[], []]
            x2_free = [[], []]
            t_sx = [None] * NCC
            t_ar = None
            for cc in range(NCC):
                sl = cc % 2
                rec = None
                for tc in range(4):
                    rec = ir.add("tensor",
                                 lambda e, sl=sl, cc=cc, tc=tc: e.transpose(
                                     pt2[sl][:, tc * P:(tc + 1) * P],
                                     res[tc][:, cc * P:(cc + 1) * P], sb_id[:, :]),
                                 waits=(([(VC.h, t_ln1[3])] + pt2_free[sl])
                                        if tc == 0 else []))
                t_tr = _IR.inc(rec, MM)
                rec = ir.add("scalar",
                             lambda e, sl=sl: e.copy(x2st[sl][:, :], pt2[sl][:, :]),
                             waits=[(MM.h, t_tr)] + x2_free[sl])
                t_x2 = _IR.inc(rec, SC)
                pt2_free[sl] = [(SC.h, t_x2)]
                t_sx[cc] = dma("gpsimd", SX[sl],
                               d_sx2[cc * P:(cc + 1) * P, hf * 512:hf * 512 + 512],
                               x2st[sl][:, :], waits=[(SC.h, t_x2)])
                rec = None
                for tc in range(4):
                    rec = ir.add("tensor",
                                 lambda e, sl=sl, cc=cc, tc=tc: e.matmul(
                                     psar[:, tc * P:tc * P + 24],
                                     x2st[sl][:, tc * P:(tc + 1) * P],
                                     sb_war[:, cc, :],
                                     start=(cc == 0 and tc == 0),
                                     stop=(cc == NCC - 1 and tc == 3)),
                                 waits=([(SC.h, t_x2)] if tc == 0 else []))
                t_ar = _IR.inc(rec, MM)
                x2_free[sl] = [(SX[sl].h, t_sx[cc]), (MM.h, t_ar)]
            t_arev = None
            for tc in range(4):
                rec = ir.add("scalar",
                             lambda e, tc=tc: e.copy(ar_sb[:, tc * 24:(tc + 1) * 24],
                                                     psar[:, tc * P:tc * P + 24]),
                             waits=[(MM.h, t_ar)])
                t_arev = _IR.inc(rec, SC)

            t_mix = [None] * 4
            t_mxf = None
            for tc in range(4):
                rec = ir.add("vector",
                             lambda e, tc=tc: e.reduce_max(
                                 nr3[:, tc:tc + 1], ar_sb[:, tc * 24 + 16:tc * 24 + 24],
                                 AX.X, negate=True),
                             waits=[(SC.h, t_arev)])
                t_nr = _IR.inc(rec, VC)
                rec = ir.add("scalar",
                             lambda e, tc=tc: e.activation(
                                 prr[:, tc * 8:(tc + 1) * 8],
                                 ar_sb[:, tc * 24 + 16:tc * 24 + 24], AF.Exp,
                                 bias=nr3[:, tc:tc + 1], scale=1.0,
                                 accum_out=rsm3[:, tc:tc + 1]),
                             waits=[(VC.h, t_nr)])
                t_rexp = _IR.inc(rec, SC)
                ir.add("vector", lambda e, tc=tc: e.reciprocal(sm3[:, 3:4],
                                                               rsm3[:, tc:tc + 1]),
                       waits=[(SC.h, t_rexp)])
                rec = ir.add("vector",
                             lambda e, tc=tc: e.tensor_scalar(
                                 prr[:, tc * 8:(tc + 1) * 8], prr[:, tc * 8:(tc + 1) * 8],
                                 sm3[:, 3:4], None, ALU.mult))
                t_prr = _IR.inc(rec, VC)
                sl = tc % 2
                rec = ir.add("tensor",
                             lambda e, tc=tc, sl=sl: e.transpose(
                                 pt2[sl][0:16, 0:P], ar_sb[:, tc * 24:tc * 24 + 16],
                                 sb_id[:, :]),
                             waits=[(SC.h, t_arev)] + pt2_free[sl])
                t_axT = _IR.inc(rec, MM)
                rec = ir.add("scalar",
                             lambda e, tc=tc, sl=sl: e.copy(axf[0:16, tc * P:(tc + 1) * P],
                                                            pt2[sl][0:16, 0:P]),
                             waits=[(MM.h, t_axT)])
                t_axe = _IR.inc(rec, SC)
                pt2_free[sl] = [(SC.h, t_axe)]
                pw = [(SC.h, t_axe)]
                if tc > 0:
                    pw.append((VC.h, t_mix[tc - 1]))
                rec = None
                for ex in range(8):
                    rec = ir.add("tensor",
                                 lambda e, tc=tc, ex=ex: e.matmul(
                                     pspe[:, ex * 16:(ex + 1) * 16],
                                     axf[0:16, tc * P:(tc + 1) * P],
                                     sb_cor[0:16, ex * 16:(ex + 1) * 16],
                                     start=(ex == 0), stop=(ex == 7)),
                                 waits=(pw if ex == 0 else []))
                t_pe = _IR.inc(rec, MM)
                ir.add("vector",
                       lambda e, tc=tc: e.tensor_scalar(
                           mix[:, tc * 16:(tc + 1) * 16], pspe[:, 0:16],
                           prr[:, tc * 8:tc * 8 + 1], 2.0, ALU.mult, ALU.mult),
                       waits=[(MM.h, t_pe), (VC.h, t_prr)])
                for ex in range(1, 8):
                    ir.add("vector",
                           lambda e, tc=tc, ex=ex: e.tensor_scalar(
                               t16[:, 0:16], pspe[:, ex * 16:(ex + 1) * 16],
                               prr[:, tc * 8 + ex:tc * 8 + ex + 1], 2.0,
                               ALU.mult, ALU.mult))
                    ir.add("vector",
                           lambda e, tc=tc: e.tensor_add(mix[:, tc * 16:(tc + 1) * 16],
                                                         mix[:, tc * 16:(tc + 1) * 16],
                                                         t16[:, 0:16]))
                rec = ir.ops["vector"][-1]
                t_mix[tc] = _IR.inc(rec, VC)
                rec = ir.add("tensor",
                             lambda e, tc=tc, sl=sl: e.transpose(
                                 pt2[sl][0:16, 0:P], mix[:, tc * 16:(tc + 1) * 16],
                                 sb_id[:, :]),
                             waits=[(VC.h, t_mix[tc])] + pt2_free[sl])
                t_mxT = _IR.inc(rec, MM)
                rec = ir.add("scalar",
                             lambda e, tc=tc, sl=sl: e.copy(mxf[0:16, tc * P:(tc + 1) * P],
                                                            pt2[sl][0:16, 0:P]),
                             waits=[(MM.h, t_mxT)])
                t_mxf = _IR.inc(rec, SC)
                pt2_free[sl] = [(SC.h, t_mxf)]
            t_mix_last = t_mix[3]
            if hf == 0:
                _stop_early(7, [(SC.h, t_mxf), (VC.h, t_mix_last)])

        with contextlib.ExitStack() as esb:
            psb = [esb.enter_context(nc.psum_tensor(f"psb{hf}_{i}", [P, 512], F32))
                   for i in range(8)]
            yb_free = [[] for _ in range(4)]
            bfree = [[(SC.h, t_mxf), (VC.h, t_mix_last)] for _ in range(8)]
            t_fev = None
            for og2 in range(4):
                for j in range(2):
                    for tc in range(4):
                        ir.add("tensor",
                               lambda e, j=j, tc=tc, og2=og2: e.matmul(
                                   psb[j * 4 + tc][:, :], sb_on[0:1, :],
                                   sb_bb[0:1, (2 * og2 + j) * 512:(2 * og2 + j + 1) * 512],
                                   start=True, stop=False),
                               waits=bfree[j * 4 + tc])
                t_mm = None
                for cc in range(NCC):
                    sl = cc % 4
                    t_y = dma("sync", LY[sl], ysb[sl][:, :],
                              d_sx2[cc * P:(cc + 1) * P, hf * 512:hf * 512 + 512],
                              waits=[(SX[cc % 2].h, t_sx[cc])] + yb_free[sl])
                    t_w = None
                    for j in range(2):
                        t_w = dma("sync", LW3[sl], wpb[sl][:, j, :],
                                  d_wb[cc, 2 * og2 + j, :, :])
                    first = [(LY[sl].h, t_y), (LW3[sl].h, t_w)]
                    rec = None
                    for j in range(2):
                        for tc in range(4):
                            rec = ir.add("tensor",
                                         lambda e, sl=sl, j=j, tc=tc, cc=cc: e.matmul(
                                             psb[j * 4 + tc][:, :],
                                             ysb[sl][:, tc * P:(tc + 1) * P],
                                             wpb[sl][:, j, :],
                                             start=False, stop=False),
                                         waits=(first if (j == 0 and tc == 0) else []))
                    t_mm = _IR.inc(rec, MM)
                    yb_free[sl] = [(MM.h, t_mm)]
                rec = None
                t_lora = [None] * 8
                for j in range(2):
                    for tc in range(4):
                        rec = ir.add("tensor",
                                     lambda e, j=j, tc=tc, og2=og2: e.matmul(
                                         psb[j * 4 + tc][:, :],
                                         mxf[0:16, tc * P:(tc + 1) * P],
                                         sb_wBT[0:16,
                                                (2 * og2 + j) * 512:(2 * og2 + j + 1) * 512],
                                         start=False, stop=True),
                                     waits=([(SC.h, t_mxf)]
                                            if (j == 0 and tc == 0) else []))
                        t_lora[j * 4 + tc] = _IR.inc(rec, MM)
                for j in range(2):
                    for tc in range(4):
                        og = 2 * og2 + j
                        rec = ir.add("vector",
                                     lambda e, j=j, tc=tc, og=og: e.tensor_add(
                                         res[tc][:, og * 512:(og + 1) * 512],
                                         res[tc][:, og * 512:(og + 1) * 512],
                                         psb[j * 4 + tc][:, :]),
                                     waits=[(MM.h, t_lora[j * 4 + tc])])
                        t_fev = _IR.inc(rec, VC)
                        bfree[j * 4 + tc] = [(VC.h, t_fev)]

        if hf == 0:
            _stop_early(8, [(VC.h, t_fev)])
        t_ln2 = [None] * 4
        for tc in range(4):
            emit_ln(tc, (VC.h, t_fev), t_ln2)
        for tc in range(4):
            dma("gpsimd", OUT, d_out[hf * 512 + tc * P:hf * 512 + (tc + 1) * P, :],
                res[tc][:, :], waits=[(VC.h, t_ln2[tc])])
    es3.close()

    ir.add("gpsimd", lambda e: e.engine_nop(), waits=[(OUT.h, OUT.v)])

    with nc.Block() as blk:
        for name in ENGS:
            _self = ir.auto.get(name)
            def runner(eng, _ops=tuple(ir.ops[name]), _self=_self):
                n = 0
                for fn, waits, incs in _ops:
                    for (hh, vv) in waits:
                        eng.wait_ge(hh, vv)
                    if _self is not None and n > 0:
                        eng.wait_ge(_self.h, n)
                    inst = fn(eng)
                    for (hh, n2) in incs:
                        inst.then_inc(hh, n2)
                    n += 1
            getattr(blk, name)(runner)
    return nc


def _tables():
    th = 1.0 / (10000.0 ** (np.arange(0, HD, 2, dtype=np.float64) / HD))
    idx = np.outer(np.arange(T, dtype=np.float64), th)
    cos2 = np.concatenate([np.cos(idx), np.cos(idx)], axis=1).T
    sin2 = np.concatenate([np.sin(idx), np.sin(idx)], axis=1).T
    sin2[:64, :] = -sin2[:64, :]
    cosf = np.ones((P, TA), dtype=np.float64)
    sinf = np.zeros((P, TA), dtype=np.float64)
    cosf[:, :T] = cos2
    sinf[:, :T] = sin2
    out = []
    for t64 in (cosf, sinf):
        hi = t64.astype(np.float32)
        lo = (t64 - hi.astype(np.float64)).astype(np.float32)
        out += [np.ascontiguousarray(hi), np.ascontiguousarray(lo)]
    return out


_NC_CACHE = [None]


def kernel(L0_attn_x, L0_attn_w_attn, L0_attn_w_proj, L0_attn_adapter_wte, L0_attn_gating,
           L0_ffn_W_base, L0_ffn_b_base, L0_ffn_A_w, L0_ffn_B_w, L0_ffn_router_w,
           L0_ffn_cores):
    f = np.float32
    x = np.ascontiguousarray(np.asarray(L0_attn_x, f))
    wat = np.asarray(L0_attn_w_attn, f)
    wpr = np.asarray(L0_attn_w_proj, f)
    adw = np.asarray(L0_attn_adapter_wte, f)
    gat = np.ascontiguousarray(np.asarray(L0_attn_gating, f).reshape(1, 1))
    wba = np.asarray(L0_ffn_W_base, f)
    bb = np.ascontiguousarray(np.asarray(L0_ffn_b_base, f).reshape(1, C))
    aw = np.asarray(L0_ffn_A_w, f)
    bw = np.asarray(L0_ffn_B_w, f)
    rw = np.asarray(L0_ffn_router_w, f)
    cor = np.asarray(L0_ffn_cores, f)

    adT = adw.T
    xfm = {}
    for b in range(B):
        xc = np.concatenate([x[b].T, adT], axis=1)
        xfm[b] = np.ascontiguousarray(
            xc.reshape(NCC, P, TA).transpose(1, 0, 2))
    wqkv = {}
    for s in range(2):
        rows = np.concatenate([wat[2048 * s:2048 * s + 2048],
                               wat[C + 2048 * s:C + 2048 * s + 2048],
                               wat[2 * C + 2048 * s:2 * C + 2048 * s + 2048]], axis=0)
        wqkv[s] = np.ascontiguousarray(
            rows.T.reshape(NCC, P, NOC, P).transpose(2, 1, 0, 3))
    wp4 = {}
    for s in range(2):
        wp4[s] = np.ascontiguousarray(
            wpr.T[2048 * s:2048 * s + 2048].reshape(16, P, 8, 512).transpose(0, 2, 1, 3))
    wb4 = np.ascontiguousarray(
        wba.T.reshape(NCC, P, 8, 512).transpose(0, 2, 1, 3))
    war = np.ascontiguousarray(
        np.concatenate([aw.T, rw.T], axis=1).reshape(NCC, P, 24).transpose(1, 0, 2))
    wBT = np.ascontiguousarray(bw.T)
    corr = np.ascontiguousarray(cor.transpose(1, 0, 2).reshape(16, P))
    ch, cl, sh, slo = _tables()
    ident = np.eye(P, dtype=f)
    ones1 = np.ones((1, P), dtype=f)
    mask = np.where(np.arange(P)[None, :] <= np.arange(P)[:, None], f(0.0), f(NEG)).astype(f)

    if _NC_CACHE[0] is None:
        _NC_CACHE[0] = _build()
    nc = _NC_CACHE[0]

    in_maps = []
    for rank in range(8):
        b, s = rank // 2, rank % 2
        in_maps.append({
            "xfm3": xfm[b],
            "xtm": np.ascontiguousarray(x[b, s * 1024:(s + 1) * 1024, :]),
            "wqkv4": wqkv[s],
            "wp4": wp4[s],
            "wb4": wb4,
            "war3": war,
            "wBT": wBT,
            "bb": bb,
            "coresr": corr,
            "gat": gat,
            "cos_hi": ch, "cos_lo": cl, "sin_hi": sh, "sin_lo": slo,
            "ident": ident, "ones1": ones1, "mask": mask,
        })
    resu = run_bass_kernel_spmd(nc, in_maps, core_ids=list(range(8)))
    out = np.empty((B, T, C), dtype=f)
    for rank in range(8):
        b, s = rank // 2, rank % 2
        out[b, s * 1024:(s + 1) * 1024, :] = resu.results[rank]["out"]
    return out
